# version 18
# baseline (speedup 1.0000x reference)
import numpy as np
import ml_dtypes
import concourse.bass as bass
import concourse.mybir as mybir
from concourse.bass_utils import run_bass_kernel_spmd
from concourse.alu_op_type import AluOpType as ALU
from contextlib import ExitStack

F32 = mybir.dt.float32
BF16 = mybir.dt.bfloat16
AF = mybir.ActivationFunctionType

T = 4096
D = 1024
NH = 16
FEAT = 16
NPHI = 153
NCH = 32
NE = 8
HID = 2048
EPS_N = 1e-6
EPS_L = 1e-6
EPS_LN = 1e-5
INV_SQRT2 = float(1.0 / np.sqrt(2.0))
WIN_SCALE = 0.125

ENGS = ("tensor", "vector", "scalar", "gpsimd", "sync")


TSP_METHS = {"tensor_scalar", "scalar_tensor_tensor", "reciprocal"}
DMA_SEMS = {"dx", "dw", "do", "dw00", "dw01", "dw10", "dw11", "dw2", "dbb0", "dbb1"}


class Plan:
    def __init__(self):
        self.q = {e: [] for e in ENGS}
        self.cnt = {}
        self.waited = {e: {} for e in ENGS}
        self.sems = {}
        self.tsp_pending = {e: None for e in ENGS}
        self.esem = {"tensor": "t", "vector": "v", "scalar": "s",
                     "gpsimd": "g", "sync": "dx"}
        self.mclass = None

    def new_sem(self, name, handle):
        self.sems[name] = handle
        self.cnt[name] = 0

    def op(self, eng, meth, *args, after=(), inc=None, **kw):
        after = list(after)
        if eng == "tensor" and meth == "matmul":
            m = args[1].shape[-1] if hasattr(args[1], "shape") else 128
            mc = 32 if m <= 32 else (64 if m <= 64 else 128)
            if self.mclass is not None and mc != self.mclass:
                self.q["tensor"].append(lambda e: e.drain())
            self.mclass = mc
        if meth != "nop" and (eng in ("vector", "scalar")
                              or (eng == "gpsimd" and meth != "dma_start")):
            sn = self.esem[eng]
            if self.cnt[sn] > 0:
                after.append((sn, self.cnt[sn]))
            if inc is None:
                inc = sn
            else:
                nm = inc[0] if isinstance(inc, tuple) else inc
                assert nm == sn, (eng, meth, inc)
        waits = []
        for tk in after:
            if tk is None:
                continue
            s, v = tk
            if s in DMA_SEMS:
                v = self.cnt[s]
            if v > self.waited[eng].get(s, 0):
                waits.append((s, v))
                self.waited[eng][s] = v
        tok = None
        incname = None
        incn = 0
        if inc is not None:
            incname, incn = inc if isinstance(inc, tuple) else (inc, 1)
            self.cnt[incname] += incn
            tok = (incname, self.cnt[incname])

        sems = self.sems

        def f(e, waits=waits, meth=meth, args=args, kw=kw, incname=incname, incn=incn):
            for s, v in waits:
                e.wait_ge(sems[s], v)
            r = getattr(e, meth)(*args, **kw)
            if incname is not None:
                r.then_inc(sems[incname], incn)
        self.q[eng].append(f)
        return tok

    def raw(self, eng, fn, after=()):
        waits = []
        for tk in after:
            if tk is None:
                continue
            s, v = tk
            if s in DMA_SEMS:
                v = self.cnt[s]
            if v > self.waited[eng].get(s, 0):
                waits.append((s, v))
                self.waited[eng][s] = v
        sems = self.sems

        def f(e, waits=waits, fn=fn):
            for s, v in waits:
                e.wait_ge(sems[s], v)
            fn(e)
        self.q[eng].append(f)

    def barrier(self):
        toks = {}
        for e in ENGS:
            own = []
            sn = self.esem.get(e)
            if e in ("vector", "scalar", "gpsimd") and self.cnt.get(sn, 0) > 0:
                own = [(sn, self.cnt[sn])]
            toks[e] = self.op(e, "nop", after=own, inc="b_" + e)
        for e in ENGS:
            self.op(e, "nop", after=[toks[o] for o in ENGS if o != e])


class Builder:
    def __init__(self, n_quarters=4, dbg=False, parts="AB", nsup=None):
        self.nq = n_quarters
        self.dbg = dbg
        self.parts = parts
        self.nsup = nsup
        self.stage = 99
        self.natt = 4
        self.castage = 99
        nc = bass.Bass("TRN2", target_bir_lowering=False)
        self.nc = nc
        self.ctx = ExitStack()
        E = self.ctx.enter_context

        self.x_h = nc.dram_tensor("x", [T, D], F32, kind="ExternalInput").ap()
        self.wq_h = nc.dram_tensor("wq", [D, D], F32, kind="ExternalInput").ap()
        self.wk_h = nc.dram_tensor("wk", [D, D], F32, kind="ExternalInput").ap()
        self.wv_h = nc.dram_tensor("wv", [D, D], F32, kind="ExternalInput").ap()
        self.wqf_h = nc.dram_tensor("wqf", [64, FEAT], F32, kind="ExternalInput").ap()
        self.wkf_h = nc.dram_tensor("wkf", [64, FEAT], F32, kind="ExternalInput").ap()
        self.wout_h = nc.dram_tensor("wout", [2 * D, D], F32, kind="ExternalInput").ap()
        self.normw_h = nc.dram_tensor("normw", [1, D], F32, kind="ExternalInput").ap()
        self.w0t_h = nc.dram_tensor("w0t", [NE, D, HID], F32, kind="ExternalInput").ap()
        self.w1t_h = nc.dram_tensor("w1t", [NE, D, HID], F32, kind="ExternalInput").ap()
        self.w2t_h = nc.dram_tensor("w2t", [NE, 17 * 128, D], F32, kind="ExternalInput").ap()
        self.b0r_h = nc.dram_tensor("b0r", [NE, 128, 16], F32, kind="ExternalInput").ap()
        self.b1r_h = nc.dram_tensor("b1r", [NE, 128, 16], F32, kind="ExternalInput").ap()
        self.gwt_h = nc.dram_tensor("gwt", [D, NE], F32, kind="ExternalInput").ap()
        self.gb_h = nc.dram_tensor("gb", [NE, 1], F32, kind="ExternalInput").ap()
        self.out_h = nc.dram_tensor("out", [T, D], F32, kind="ExternalOutput").ap()
        if dbg:
            self.dt1_h = nc.dram_tensor("d_t1", [T, D], F32, kind="ExternalOutput").ap()
            self.dg_h = nc.dram_tensor("d_g", [T, NE], F32, kind="ExternalOutput").ap()

        self.t1 = E(nc.sbuf_tensor("t1", [128, 8 * D], F32))
        self.kv = E(nc.sbuf_tensor("kv", [128, NH * 65], F32))
        self.kv2 = E(nc.sbuf_tensor("kv2", [32, NH * 65], F32))
        self.kvb = E(nc.sbuf_tensor("kvb", [128, NH * 65], BF16))
        self.kvb2 = E(nc.sbuf_tensor("kvb2", [128, NH * 65], BF16))
        self.ktail = E(nc.sbuf_tensor("ktail", [128, 8 * 128], BF16))
        self.vtail = E(nc.sbuf_tensor("vtail", [128, NH * 65], BF16))
        self.normw_b = E(nc.sbuf_tensor("normw_b", [128, D], BF16))
        self.wqf_s = E(nc.sbuf_tensor("wqf_s", [128, 2 * FEAT], BF16))
        self.wkf_s = E(nc.sbuf_tensor("wkf_s", [128, 2 * FEAT], BF16))
        self.r0ones = E(nc.sbuf_tensor("r0ones", [128, 512], BF16))
        self.maskC = E(nc.sbuf_tensor("maskC", [128, 128], BF16))
        self.maskP = E(nc.sbuf_tensor("maskP", [128, 128], BF16))
        self.maskL = E(nc.sbuf_tensor("maskL", [128, 128], BF16))
        self.idbf = E(nc.sbuf_tensor("idbf", [128, 128], BF16))
        self.gwt_s = E(nc.sbuf_tensor("gwt_s", [128, 8 * NE], BF16))
        self.gbb = E(nc.sbuf_tensor("gbb", [128, NE], F32))
        self.smal = E(nc.sbuf_tensor("smal", [128, 32], F32))
        self.top2 = E(nc.sbuf_tensor("top2", [128, 4 * NE], F32))

        AR_KB = 155
        self.arena = E(nc.sbuf_tensor("arena", [128, AR_KB * 512], BF16))
        ar = self.arena.ap()

        def carve(off_kb, nelem, dt, parts=128):
            offb = int(off_kb * 1024)
            if dt == F32:
                assert offb % 4 == 0
                return ar.bitcast(F32)[0:parts, offb // 4: offb // 4 + nelem]
            return ar[0:parts, offb // 2: offb // 2 + nelem]

        self.wq_s = carve(0, 8 * D, BF16)
        self.wk_s = carve(16, 8 * D, BF16)
        self.wv_s = carve(32, 8 * D, BF16)
        self.wout_s = carve(48, 16 * D, BF16)
        self.hT = carve(80, 8 * 512, BF16)
        self.q_dm = carve(88, 8 * 512, BF16)
        self.k_sup = carve(96, 8 * 512, BF16)
        self.v_sup = carve(104, 4 * NH * 65, BF16)
        self.hbf = carve(112.125, D, BF16)
        self.qf_tm = carve(114.125, 256, F32)
        self.qf_h = carve(115.125, 256, F32)
        self.kf_tm = carve(116.125, 256, F32)
        self.kf_h = carve(117.125, 256, F32)
        self.kphi = carve(118.125, NH * NPHI, BF16)
        self.qphi = carve(122.9375, NH * NPHI, BF16)
        self.qphiT = carve(127.75, 4 * 128, BF16)
        self.qphiT2 = carve(128.75, 4 * 128, BF16)
        self.kphiT = carve(129.75, 4 * 128, BF16)
        self.kphiT2 = carve(130.75, 4 * 128, BF16)
        self.sqA = carve(131.75, 4 * 128, BF16)
        self.eC = carve(132.75, 4 * 128, BF16)
        self.eP = carve(133.75, 4 * 128, BF16)
        self.hbf2 = carve(132.75, D, BF16)
        self.concat = carve(134.75, 2 * D, BF16)
        self.concatT = self.concat
        self.fscratch = carve(138.75, D, F32)
        self.sqscratch = carve(138.75, D, BF16)
        self.q_pad = carve(142.75, NH * 128, BF16)
        self.nwrow = carve(142.75, D, BF16)
        self.xring = [carve(146.75 + 4 * i, D, F32) for i in range(2)]
        self.pw0 = carve(0, 8 * HID, BF16)
        self.pw1 = carve(32, 8 * HID, BF16)
        self.pw2 = carve(64, 17 * D, BF16)
        self.t1T = carve(98, 8 * D, BF16)
        self.act_s = carve(114, 16 * 512, BF16)
        self.h0b_t = [carve(130 + i, 512, BF16) for i in range(2)]
        self.silu_t = [carve(132 + i, 512, BF16) for i in range(2)]
        self.fscrB = carve(134, D, F32)
        self.t1bf = carve(138, D, BF16)
        self.b0col = carve(140, 16, F32)
        self.b1col = carve(140.0625, 16, F32)
        self.scr8 = carve(140.125, NE, F32)
        self.G_s = carve(140.25, 8 * NE, F32)

        self.pbA = E(nc.psum_tensor("pbA", [128, 1024], F32))
        self.pbB = E(nc.psum_tensor("pbB", [128, 1024], F32))
        self.pb2 = E(nc.psum_tensor("pb2", [128, 512], F32))
        self.pb3 = E(nc.psum_tensor("pb3", [128, 512], F32))
        self.pb6 = E(nc.psum_tensor("pb6", [128, 512], F32))
        self.pT = E(nc.psum_tensor("pT", [128, 1024], BF16))
        self.pb01 = self.pbA

        self.pl = Plan()
        for sn in (["t", "v", "s", "g", "dx", "dw", "do",
                    "dw00", "dw01", "dw10", "dw11", "dw2", "dbb0", "dbb1"]
                   + ["b_" + e for e in ENGS]):
            self.pl.new_sem(sn, E(nc.semaphore("sem_" + sn)))

        from concourse.hw_specs import get_activation_tables
        tabs = list(get_activation_tables(nc.m.arch).values())

        def set_id(need):
            for i, fs in enumerate(tabs):
                if all(f in fs for f in need):
                    return i
            raise ValueError(need)
        self.SET_NLE = set_id([AF.Exp, AF.Ln, AF.Square, AF.Identity])
        self.SET_SILU = set_id([AF.Silu, AF.Identity])

        self.lr = {k: None for k in ["pb01", "pbBwin", "pbBlin", "pb2", "pb3", "pb6", "pT"]}
        self.x_tok = [None] * NCH
        self.xfree = [None, None]
        self.kvb_tok = [None, None]
        self.ln_tok = [None] * 8
        self.t2x_tok = [None] * 8
        self.pending_tail = None
        self.pending_ln = None
        self.tk_wq = self.tk_wk = self.tk_wv = self.tk_wo = None
        self.sup_done = None
        self.ktail_tok = None
        self.vtail_tok = None
        self.prev_tts = [None, None]
        self.hbf_sl = [self.hbf, self.hbf2]
        self.prev_kd = None
        self.prev_yl = None
        self.prev_yw = None
        self.prev_ct = None
        self.prev_wo = None
        self.prev_qpad_read = None
        self.prev_phig = None
        self.store_tok = None
        self.dbg_toks = []

    def prelude(self):
        OP, tsc, col = self.OP, self.tsc, self.col
        mvals = self.fscratch[0:128, 0:128]
        mvals2 = self.fscratch[0:128, 128:256]
        tki1 = OP("gpsimd", "iota", mvals, pattern=[[1, 128]], base=0,
                  channel_multiplier=-1, allow_small_or_imprecise_dtypes=True, inc="g")
        tki2 = OP("gpsimd", "iota", mvals2, pattern=[[-1, 128]], base=-64,
                  channel_multiplier=1, allow_small_or_imprecise_dtypes=True, inc="g")
        tsc("vector", self.maskL[:, :], mvals, 0.0, None, ALU.is_ge, after=[tki1])
        tsc("vector", self.maskC[:, :], mvals, 64.0, None, ALU.is_le)
        OP("vector", "tensor_tensor", self.maskC[:, :], self.maskC[:, :],
           self.maskL[:, :], ALU.mult)
        tsc("vector", self.maskP[:, :], mvals2[:, :], 0.0, None, ALU.is_ge,
            after=[tki2])
        tkid = tsc("vector", self.idbf[:, :], mvals, 0.0, None, ALU.is_equal, inc="v")
        self.tk_idbf = tkid
        tki3 = OP("gpsimd", "iota", mvals, pattern=[[0, 128]], base=0,
                  channel_multiplier=1, allow_small_or_imprecise_dtypes=True,
                  after=[tkid], inc="g")
        t_r0a = tsc("vector", self.r0ones[:, 0:128], mvals, 0.0, None, ALU.is_equal,
                    after=[tki3], inc="v")
        OP("vector", "tensor_copy", self.r0ones[:, 128:256], self.r0ones[:, 0:128])
        self.tk_r0 = OP("vector", "tensor_copy", self.r0ones[:, 256:512],
                        self.r0ones[:, 0:256], inc="v")
        OP("vector", "memset", self.qphiT2[:, :], 0.0)
        OP("vector", "memset", self.kphiT2[:, :], 0.0)
        OP("vector", "memset",
           self.v_sup.rearrange("p (h c) -> p h c", c=65)[:, :, 64:65], 1.0)
        OP("vector", "memset",
           self.vtail.rearrange("p (h c) -> p h c", c=65)[:, :, 64:65], 1.0)
        OP("vector", "memset", self.t1[:, :], 0.0)
        OP("vector", "memset", self.kv[:, :], 0.0)
        OP("vector", "memset", self.kv2[0:25, :], 0.0)
        self.kvb_tok[0] = OP("vector", "memset", self.kvb[:, :], 0.0, inc="v")
        self.kvb_tok[1] = OP("vector", "memset", self.kvb2[:, :], 0.0, inc="v")

        self.tk_wq = self.load_mat(self.wq_s, self.wq_h, 8)
        self.tk_wk = self.load_mat(self.wk_s, self.wk_h, 8)
        self.tk_wv = self.load_mat(self.wv_s, self.wv_h, 8)
        self.tk_wo = self.load_mat(self.wout_s, self.wout_h, 16)
        t_z1 = OP("vector", "memset", self.wqf_s[:, :], 0.0, inc="v")
        t_z2 = OP("vector", "memset", self.wkf_s[:, :], 0.0, inc="v")
        t1_ = OP("gpsimd", "dma_start", out=self.wqf_s[0:64, 0:FEAT],
                 in_=self.wqf_h[:, :], after=[t_z1], inc=("dw", 16))
        t2_ = OP("gpsimd", "dma_start", out=self.wqf_s[64:128, FEAT:2 * FEAT],
                 in_=self.wqf_h[:, :], inc=("dw", 16))
        self.tk_wqf = t2_
        t3_ = OP("gpsimd", "dma_start", out=self.wkf_s[0:64, 0:FEAT],
                 in_=self.wkf_h[:, :], after=[t_z2], inc=("dw", 16))
        t4_ = OP("gpsimd", "dma_start", out=self.wkf_s[64:128, FEAT:2 * FEAT],
                 in_=self.wkf_h[:, :], inc=("dw", 16))
        self.tk_wkf = t4_
        self.tk_gw = OP("gpsimd", "dma_start",
                        out=self.gwt_s.rearrange("p (s w) -> p s w", s=8),
                        in_=self.gwt_h.rearrange("(s p) w -> p s w", p=128),
                        inc=("dw", 16))
        t_zn = OP("vector", "memset", self.nwrow[:, :], 0.0, inc="v")
        tk_nw = OP("gpsimd", "dma_start", out=self.nwrow[0:1, :],
                   in_=self.normw_h[:, :], after=[t_zn], inc=("dw", 16))
        t_zg = OP("vector", "memset", self.t1bf[:, 0:NE], 0.0, inc="v")
        tk_gbl = OP("gpsimd", "dma_start", out=self.t1bf[0:1, 0:NE],
                    in_=self.gb_h.rearrange("e one -> one e"), after=[t_zg],
                    inc=("dw", 16))
        OP("tensor", "matmul", self.pb01[:, 0:512], self.r0ones[:, 0:128],
           self.nwrow[:, 0:512], start=True, stop=True,
           after=[tk_nw, self.tk_r0, self.kvb_tok[1]])
        tnb = OP("tensor", "matmul", self.pb01[:, 512:1024], self.r0ones[:, 0:128],
                 self.nwrow[:, 512:1024], start=True, stop=True, inc="t")
        self.tk_nbe = OP("vector", "tensor_copy", self.normw_b[:, :], self.pb01[:, :],
                         after=[tnb], inc="v")
        OP("vector", "memset", self.q_pad[:, :], 0.0, after=[tnb])
        tgb = OP("tensor", "matmul", self.pb2[:, 0:NE], self.r0ones[:, 0:128],
                 self.t1bf[:, 0:NE], start=True, stop=True, after=[tk_gbl], inc="t")
        self.tk_gb = OP("vector", "tensor_copy", self.gbb[:, :], self.pb2[:, 0:NE],
                        after=[tgb], inc="v")
        self.lr["pb01"] = self.tk_nbe
        self.lr["pb2"] = self.tk_gb
        self.load_act(self.SET_NLE)
        self.load_x(0)
        self.load_x(1)
        self.pl.barrier()

    def superchunk(self, S):
        OP = self.OP
        hT3 = self.hT.rearrange("p (s w) -> p s w", s=8)
        h_toks = []
        for cpair in range(2):
            cs = [2 * cpair, 2 * cpair + 1]
            t_sqs = {}
            t_rss = {}
            for c in cs:
                C = 4 * S + c
                xa = self.xring[C % 2]
                t_sqs[c] = OP("scalar", "activation", self.sqscratch[:, :],
                              xa[:, :], AF.Square, accum_out=self.col(24 + c),
                              after=[self.x_tok[C], self.sup_done], inc="s")
            t_mss = {}
            for c in cs:
                t_mss[c] = self.tsc("vector", self.col(28 + c), self.col(24 + c),
                                    1.0 / D, EPS_N, ALU.mult, ALU.add,
                                    after=[t_sqs[c]], inc="v")
            for c in cs:
                t_ln = OP("scalar", "activation", self.col(24 + c),
                          self.col(28 + c), AF.Ln, after=[t_mss[c]], inc="s")
                t_rss[c] = OP("scalar", "activation", self.col(28 + c),
                              self.col(24 + c), AF.Exp, scale=-0.5, inc="s")
            for c in cs:
                C = 4 * S + c
                xa = self.xring[C % 2]
                hb = self.hbf_sl[c % 2]
                t_h1 = OP("scalar", "activation", hb[:, :], xa[:, :], AF.Identity,
                          scale=self.col(28 + c),
                          after=[t_rss[c], self.prev_tts[c % 2]], inc="s")
                cq = C % 8
                t_2x = OP("scalar", "activation", self.t1[:, D * cq:D * (cq + 1)],
                          xa[:, :], AF.Identity, scale=2.0,
                          after=[self.store_tok, self.t1_war], inc="s")
                self.t2x_tok[cq] = t_2x
                self.xfree[C % 2] = t_2x
                self.load_x(C + 2)
                t_h2 = OP("vector", "tensor_tensor", hb[:, :], hb[:, :],
                          self.normw_b[:, :], ALU.mult, after=[t_h1, self.tk_nbe],
                          inc="v")
                tts = None
                for j in range(8):
                    tts = OP("tensor", "transpose",
                             self.pT[:, 128 * j:128 * (j + 1)],
                             hb[:, 128 * j:128 * (j + 1)], self.idbf[:, :],
                             after=[t_h2, self.tk_idbf, self.lr["pT"]],
                             inc=("t" if j == 7 else None))
                self.prev_tts[c % 2] = tts
                t_ev = OP("vector", "tensor_copy",
                          hT3[:, :, 128 * c:128 * (c + 1)],
                          self.pT.rearrange("p (s w) -> p s w", s=8),
                          after=[tts, self.sup_done], inc="v")
                self.lr["pT"] = t_ev
                h_toks.append(t_ev)
        h_all = h_toks[-1]
        if self.stage < 2:
            return
        pp = [self.lr["pb01"], self.lr["pb01"]]
        wq3 = self.wq_s.rearrange("p (s w) -> p s w", s=8)
        wk3 = self.wk_s.rearrange("p (s w) -> p s w", s=8)
        wv3 = self.wv_s.rearrange("p (s w) -> p s w", s=8)
        k3 = self.k_sup.rearrange("p (s w) -> p s w", s=8)
        for mj in range(8):
            hf = mj % 2
            tk = None
            for kk in range(8):
                tk = OP("tensor", "matmul", self.pb01[:, 512 * hf:512 * (hf + 1)],
                        wq3[:, kk, 128 * mj:128 * (mj + 1)], hT3[:, kk, :],
                        start=(kk == 0), stop=(kk == 7),
                        after=[self.tk_wq, h_all, pp[hf]],
                        inc=("t" if kk == 7 else None))
            pp[hf] = OP("vector", "tensor_copy", self.q_dm[:, 512 * mj:512 * (mj + 1)],
                        self.pb01[:, 512 * hf:512 * (hf + 1)],
                        after=[tk, self.sup_done], inc="v")
        tk_q = (pp[0], pp[1])
        for mj in range(8):
            hf = mj % 2
            tk = None
            for kk in range(8):
                tk = OP("tensor", "matmul", self.pb01[:, 512 * hf:512 * (hf + 1)],
                        wk3[:, kk, 128 * mj:128 * (mj + 1)], hT3[:, kk, :],
                        start=(kk == 0), stop=(kk == 7),
                        after=[self.tk_wk, pp[hf]],
                        inc=("t" if kk == 7 else None))
            pp[hf] = OP("vector", "tensor_copy", self.k_sup[:, 512 * mj:512 * (mj + 1)],
                        self.pb01[:, 512 * hf:512 * (hf + 1)],
                        after=[tk, self.sup_done], inc="v")
        tk_k = (pp[0], pp[1])
        v65 = self.v_sup.rearrange("p (c h e) -> p c h e", c=4, e=65)
        for c in range(4):
            for nh in range(2):
                hf = nh
                tk = None
                for kk in range(8):
                    tk = OP("tensor", "matmul", self.pb01[:, 512 * hf:512 * (hf + 1)],
                            hT3[:, kk, 128 * c:128 * (c + 1)],
                            wv3[:, kk, 512 * nh:512 * (nh + 1)],
                            start=(kk == 0), stop=(kk == 7),
                            after=[self.tk_wv, pp[hf]],
                            inc=("t" if kk == 7 else None))
                pp[hf] = OP("vector", "tensor_copy",
                            v65[:, c, 8 * nh:8 * (nh + 1), 0:64],
                            self.pb01[:, 512 * hf:512 * (hf + 1)]
                                .rearrange("p (h e) -> p h e", e=64),
                            after=[tk, self.sup_done], inc="v")
        tk_v = (pp[0], pp[1])
        self.lr["pb01"] = pp[1]
        if self.stage < 3:
            return
        for c in range(min(4, self.natt)):
            self.chunk_attn(S, c, 4 * S + c, tk_q, tk_k, tk_v)
        kt = OP("vector", "tensor_copy",
                self.ktail.rearrange("p (s w) -> p s w", s=8),
                k3[:, :, 384:512], inc="v")
        vt = OP("vector", "tensor_copy", self.vtail[:, :],
                self.v_sup[:, 3 * NH * 65:4 * NH * 65], inc="v")
        self.ktail_tok = kt
        self.vtail_tok = vt
        self.sup_done = self.prev_kd

    def chunk_attn(self, S, c, C, tk_q, tk_k, tk_v):
        OP, tsc = self.OP, self.tsc
        vr = self.v_sup[:, c * NH * 65:(c + 1) * NH * 65]
        first_c = (c == 0)
        vprev = self.vtail if first_c else \
            self.v_sup[:, (c - 1) * NH * 65:c * NH * 65]

        def kslice(h_or_sl, width=128, prev=False, whole_slice=True, off=0):
            sl = h_or_sl
            if prev and first_c:
                return self.ktail[:, 128 * sl + off:128 * sl + off + width]
            base = 512 * sl + 128 * (c - 1 if prev else c)
            return self.k_sup[:, base + off:base + off + width]

        qp3 = self.q_pad.rearrange("p (s two w) -> p s two w", s=8, two=2)
        qd3 = self.q_dm.rearrange("p (s w) -> p s w", s=8)
        OP("gpsimd", "tensor_copy", qp3[0:64, :, 0, :],
           qd3[0:64, :, 128 * c:128 * (c + 1)],
           after=[tk_q[0], tk_q[1], self.prev_qpad_read])
        tqp = OP("gpsimd", "tensor_copy", qp3[64:128, :, 1, :],
                 qd3[64:128, :, 128 * c:128 * (c + 1)])
        tk1 = None
        for sl in range(8):
            tk1 = OP("tensor", "matmul", self.pb2[:, 32 * sl:32 * (sl + 1)],
                     self.q_dm[:, 512 * sl + 128 * c:512 * sl + 128 * (c + 1)],
                     self.wqf_s[:, :], start=True, stop=True,
                     after=[tk_q[0], tk_q[1], self.tk_wqf, self.lr["pb2"]],
                     inc=("t" if sl == 7 else None))
        t_qftm = OP("vector", "tensor_copy", self.qf_tm[:, :], self.pb2[:, 0:256],
                    after=[tk1], inc="v")
        tk2 = None
        for sl in range(8):
            tk2 = OP("tensor", "matmul", self.pb2[:, 256 + 32 * sl:256 + 32 * (sl + 1)],
                     kslice(sl), self.wkf_s[:, :], start=True, stop=True,
                     after=[tk_k[0], tk_k[1], self.tk_wkf, t_qftm],
                     inc=("t" if sl == 7 else None))
        t_kftm = OP("vector", "tensor_copy", self.kf_tm[:, :], self.pb2[:, 256:512],
                    after=[tk2], inc="v")
        self.lr["pb2"] = t_kftm
        self.emit_tail()
        if self.castage < 1:
            return
        if self.castage < 2:
            return
        tsc("vector", self.qf_h[:, :], self.qf_tm[:, :], INV_SQRT2, None, ALU.mult,
            after=[self.prev_phig])
        t_sc = tsc("vector", self.kf_h[:, :], self.kf_tm[:, :], INV_SQRT2, None,
                   ALU.mult, inc="v")

        k3d = self.kphi.rearrange("p (h w) -> p h w", h=NH)
        q3d = self.qphi.rearrange("p (h w) -> p h w", h=NH)
        ktm3 = self.kf_tm.rearrange("p (h w) -> p h w", h=NH)
        qtm3 = self.qf_tm.rearrange("p (h w) -> p h w", h=NH)
        khh3 = self.kf_h.rearrange("p (h w) -> p h w", h=NH)
        qhh3 = self.qf_h.rearrange("p (h w) -> p h w", h=NH)
        OP("vector", "memset", k3d[:, :, 0:1], 1.0, after=[self.prev_kd])
        OP("vector", "memset", q3d[:, :, 0:1], 1.0)
        OP("vector", "tensor_copy", k3d[:, :, 1:17], ktm3[:, :, :])
        OP("vector", "tensor_copy", q3d[:, :, 1:17], qtm3[:, :, :])
        def _two_sided(ap_k, ap_q):
            d = ap_q.offset - ap_k.offset
            return bass.AP(tensor=ap_k.tensor, offset=ap_k.offset,
                           ap=[list(p) for p in ap_k.ap[:1]] +
                              [[d, 2]] + [list(p) for p in ap_k.ap[1:]])
        off = 17
        for i in range(FEAT):
            n = FEAT - i
            if n > 1:
                outv = _two_sided(k3d[:, :, off + 1:off + n],
                                  q3d[:, :, off + 1:off + n])
                in0v = _two_sided(
                    khh3[:, :, i:i + 1].broadcast_to([128, NH, n - 1]),
                    qhh3[:, :, i:i + 1].broadcast_to([128, NH, n - 1]))
                in1v = _two_sided(ktm3[:, :, i + 1:FEAT], qtm3[:, :, i + 1:FEAT])
                OP("vector", "tensor_tensor", outv, in0v, in1v, ALU.mult)
            off += n
        off = 17
        t_phiv = None
        for i in range(FEAT):
            n = FEAT - i
            outd = _two_sided(k3d[:, :, off:off + 1], q3d[:, :, off:off + 1])
            ind = _two_sided(khh3[:, :, i:i + 1], qhh3[:, :, i:i + 1])
            t_phiv = OP("vector", "tensor_tensor", outd, ind, ind, ALU.mult,
                        inc=("v" if i == FEAT - 1 else None))
            off += n
        t_phig = t_phiv
        self.prev_phig = t_phig
        t_phi = t_phiv
        def ywin_bank(hg):
            return (self.pb2, 0, "pb2") if hg % 2 == 0 else (self.pbB, 512, "pbBwin")

        def emit_win_norm(hg, tyw):
            pb, base, key = ywin_bank(hg)
            zwin = pb[:, base:base + 260].rearrange(
                "p (j w) -> p j w", w=65)[:, :, 64:65]
            tlz = OP("scalar", "activation", self.smal[:, 20:24],
                     zwin.squeeze(2), AF.Ln, after=[tyw], inc="s")
            rcw = OP("scalar", "activation", self.smal[:, 16:20],
                     self.smal[:, 20:24], AF.Exp, scale=-1.0, inc="s")
            c3 = self.concat.rearrange("p (half h w) -> p half h w", half=2, w=64)
            tnw = None
            for j in range(4):
                tnw = OP("scalar", "activation", c3[:, 1, 4 * hg + j, :],
                         pb[:, base + 65 * j:base + 65 * j + 64],
                         AF.Identity, scale=self.smal[:, 16 + j:17 + j],
                         after=[rcw, self.prev_ct], inc="s")
            self.lr[key] = tnw
            return tnw
        win_toks = []
        pend_norm = None
        for hg in range(4):
            hs4 = [4 * hg + j for j in range(4)]
            tsC = None
            for j, h in enumerate(hs4):
                tsC = OP("tensor", "matmul", self.pb3[:, 128 * j:128 * (j + 1)],
                         kslice(h // 2), self.q_pad[:, 128 * h:128 * (h + 1)],
                         start=True, stop=True,
                         after=[tqp, tk_k[0], tk_k[1], self.lr["pb3"]],
                         inc=("t" if j == 3 else None))
            teC = OP("scalar", "activation", self.eC[:, :], self.pb3[:, :],
                     AF.Exp, scale=WIN_SCALE, after=[tsC, self.prev_yw], inc="s")
            teCm = OP("gpsimd", "tensor_tensor",
                      self.eC.rearrange("p (j w) -> p j w", j=4),
                      self.eC.rearrange("p (j w) -> p j w", j=4),
                      self.maskC[:, :].unsqueeze(1).broadcast_to([128, 4, 128]),
                      ALU.mult, after=[teC])
            self.lr["pb3"] = teCm
            tePm = None
            if C > 0:
                tsP = None
                for j, h in enumerate(hs4):
                    tsP = OP("tensor", "matmul",
                             self.pb6[:, 128 * j:128 * (j + 1)],
                             kslice(h // 2, prev=True),
                             self.q_pad[:, 128 * h:128 * (h + 1)],
                             start=True, stop=True,
                             after=[self.lr["pb6"],
                                    self.ktail_tok if first_c else None],
                             inc=("t" if j == 3 else None))
                teP = OP("scalar", "activation", self.eP[:, :], self.pb6[:, :],
                         AF.Exp, scale=WIN_SCALE, after=[tsP, self.prev_yw], inc="s")
                tePm = OP("gpsimd", "tensor_tensor",
                          self.eP.rearrange("p (j w) -> p j w", j=4),
                          self.eP.rearrange("p (j w) -> p j w", j=4),
                          self.maskP[:, :].unsqueeze(1).broadcast_to([128, 4, 128]),
                          ALU.mult, after=[teP])
                self.lr["pb6"] = tePm
            self.prev_qpad_read = tsP if C > 0 else tsC
            tyw = None
            ywpb, ywbase, ywkey = ywin_bank(hg)
            for j, h in enumerate(hs4):
                tk = OP("tensor", "matmul",
                        ywpb[:, ywbase + 65 * j:ywbase + 65 * (j + 1)],
                        self.eC[:, 128 * j:128 * (j + 1)], vr[:, 65 * h:65 * (h + 1)],
                        start=True, stop=(C == 0),
                        after=[teCm, tk_v[0], tk_v[1], self.lr[ywkey]],
                        inc=("t" if (C == 0 and j == 3) else None))
                if C > 0:
                    tk = OP("tensor", "matmul",
                            ywpb[:, ywbase + 65 * j:ywbase + 65 * (j + 1)],
                            self.eP[:, 128 * j:128 * (j + 1)],
                            vprev[:, 65 * h:65 * (h + 1)],
                            start=False, stop=True,
                            after=[tePm, self.vtail_tok if first_c else None],
                            inc=("t" if j == 3 else None))
                tyw = tk
            self.prev_yw = tyw
            if pend_norm is not None:
                win_toks.append(emit_win_norm(*pend_norm))
            pend_norm = (hg, tyw)
        win_toks.append(emit_win_norm(*pend_norm))
        if self.castage < 5:
            return
        concat_last = None
        t_kd2 = None
        for hg in range(4):
            hs4 = [4 * hg + j for j in range(4)]
            for phi_src, dT, dT2 in ((self.qphi, self.qphiT, self.qphiT2),
                                     (self.kphi, self.kphiT, self.kphiT2)):
                ttr = None
                for j, h in enumerate(hs4):
                    ttr = OP("tensor", "transpose", self.pT[:, 128 * j:128 * (j + 1)],
                             phi_src[:, NPHI * h:NPHI * h + 128], self.idbf[:, :],
                             after=[t_phi, t_phig, self.lr["pT"], self.prev_yl])
                for j, h in enumerate(hs4):
                    ttr = OP("tensor", "transpose",
                             self.pT[0:25, 512 + 128 * j:512 + 128 * (j + 1)],
                             phi_src[:, NPHI * h + 128:NPHI * h + 153], self.idbf[:, :],
                             inc=("t" if j == 3 else None))
                OP("scalar", "activation", dT[:, :], self.pT[:, 0:512], AF.Identity,
                   after=[ttr, self.prev_yl])
                t_pe = OP("scalar", "activation", dT2[0:25, :],
                          self.pT[0:25, 512:1024], AF.Identity, inc="s")
                self.lr["pT"] = t_pe
            tqk = None
            for j in range(4):
                OP("tensor", "matmul", self.pb3[:, 128 * j:128 * (j + 1)],
                   self.kphiT[:, 128 * j:128 * (j + 1)],
                   self.qphiT[:, 128 * j:128 * (j + 1)],
                   start=True, stop=False, after=[t_pe, self.lr["pb3"]])
                tqk = OP("tensor", "matmul", self.pb3[:, 128 * j:128 * (j + 1)],
                         self.kphiT2[:, 128 * j:128 * (j + 1)],
                         self.qphiT2[:, 128 * j:128 * (j + 1)],
                         start=False, stop=True,
                         inc=("t" if j == 3 else None))
            tAe = OP("scalar", "activation", self.sqA[:, :], self.pb3[:, :],
                     AF.Identity, after=[tqk, self.prev_yl], inc="s")
            tAm = OP("gpsimd", "tensor_tensor",
                     self.sqA.rearrange("p (j w) -> p j w", j=4),
                     self.sqA.rearrange("p (j w) -> p j w", j=4),
                     self.maskL[:, :].unsqueeze(1).broadcast_to([128, 4, 128]),
                     ALU.mult, after=[tAe])
            self.lr["pb3"] = tAe
            tyl = None
            for j, h in enumerate(hs4):
                OP("tensor", "matmul", self.pbB[:, 65 * j:65 * (j + 1)],
                   self.sqA[:, 128 * j:128 * (j + 1)], vr[:, 65 * h:65 * (h + 1)],
                   start=True, stop=False,
                   after=[tAm, self.lr["pbBlin"]])
                OP("tensor", "matmul", self.pbB[:, 65 * j:65 * (j + 1)],
                   self.qphiT[:, 128 * j:128 * (j + 1)],
                   self.kvb[:, 65 * h:65 * (h + 1)],
                   start=False, stop=False, after=[self.kvb_tok[0]])
                tyl = OP("tensor", "matmul", self.pbB[:, 65 * j:65 * (j + 1)],
                         self.qphiT2[:, 128 * j:128 * (j + 1)],
                         self.kvb2[:, 65 * h:65 * (h + 1)],
                         start=False, stop=True, after=[self.kvb_tok[1]],
                         inc=("t" if j == 3 else None))
            self.prev_yl = tyl
            if self.castage < 8:
                continue
            t_kd = None
            for j, h in enumerate(hs4):
                t_kd = OP("tensor", "matmul", self.pb6[:, 65 * j:65 * (j + 1)],
                          self.kphi[:, NPHI * h:NPHI * h + 128],
                          vr[:, 65 * h:65 * (h + 1)],
                          start=True, stop=True,
                          after=[t_phi, t_phig, tyl, self.lr["pb6"]],
                          inc=("t" if j == 3 else None))
            t_kd2 = None
            for j, h in enumerate(hs4):
                t_kd2 = OP("tensor", "matmul", self.pb3[0:25, 65 * j:65 * (j + 1)],
                           self.kphi[:, NPHI * h + 128:NPHI * h + 153],
                           vr[:, 65 * h:65 * (h + 1)],
                           start=True, stop=True, after=[self.lr["pb3"]],
                           inc=("t" if j == 3 else None))
            if self.castage < 9:
                continue
            zlin = self.pbB[:, 0:260].rearrange("p (j w) -> p j w", w=65)[:, :, 64:65]
            tsc("vector", self.smal[:, 8:12], zlin.squeeze(2), 1.0, EPS_L,
                ALU.mult, ALU.add, after=[tyl])
            rc = OP("vector", "reciprocal", self.smal[:, 12:16], self.smal[:, 8:12],
                    inc="v")
            c3 = self.concat.rearrange("p (half h w) -> p half h w", half=2, w=64)
            tnl = OP("vector", "tensor_tensor", c3[:, 0, 4 * hg:4 * hg + 4, :],
                     self.pbB[:, 0:260].rearrange("p (j w) -> p j w", w=65)[:, :, 0:64],
                     self.smal[:, 12:16].unsqueeze(2).broadcast_to([128, 4, 64]),
                     ALU.mult, after=[rc, self.prev_ct], inc="v")
            self.lr["pbBlin"] = tnl
            concat_last = tnl
            kv3 = self.kv.rearrange("p (h e) -> p h e", e=65)
            kv23 = self.kv2.rearrange("p (h e) -> p h e", e=65)
            OP("vector", "tensor_tensor", kv3[:, 4 * hg:4 * hg + 4, :],
               kv3[:, 4 * hg:4 * hg + 4, :],
               self.pb6[:, 0:260].rearrange("p (j w) -> p j w", w=65),
               ALU.add, after=[t_kd])
            tkva = OP("vector", "tensor_tensor", kv23[0:25, 4 * hg:4 * hg + 4, :],
                      kv23[0:25, 4 * hg:4 * hg + 4, :],
                      self.pb3[0:25, 0:260].rearrange("p (j w) -> p j w", w=65),
                      ALU.add, after=[t_kd2], inc="v")
            self.lr["pb6"] = tkva
            self.lr["pb3"] = tkva
        if self.castage < 10:
            return
        self.prev_kd = t_kd2
        self.emit_ln()
        self.pending_tail = (C, concat_last, win_toks[-1], self.prev_yl)

    def OP(self, *a, **k):
        return self.pl.op(*a, **k)

    def tsc(self, eng, out, in0, s1, s2, op0, op1=None, after=(), inc=None, accum=None):
        if accum is not None:
            return self.OP(eng, "tensor_scalar", out, in0, s1, None, op0, op1,
                           accum_out=accum, after=after, inc=inc)
        if op1 is None:
            return self.OP(eng, "tensor_scalar", out, in0, s1, None, op0,
                           after=after, inc=inc)
        return self.OP(eng, "tensor_scalar", out, in0, s1, s2, op0, op1,
                       after=after, inc=inc)

    def col(self, i):
        return self.smal[:, i:i + 1]

    def load_act(self, set_id, after=()):
        nc = self.nc

        def fn(e):
            e.add_instruction(mybir.InstLoadActFuncSet(
                name=nc.get_next_instruction_name(),
                act_func_set_id=set_id, ins=[], outs=[]))
        self.pl.raw("scalar", fn, after=after)

    def load_mat(self, dst, src, nslice, after=()):
        return self.OP("gpsimd", "dma_start",
                       out=dst.rearrange("p (s w) -> p s w", s=nslice),
                       in_=src.rearrange("(s p) w -> p s w", p=128),
                       after=after, inc=("dw", 16))

    def load_x(self, C, after_extra=()):
        if C >= self.nq * 8:
            return
        self.x_tok[C] = self.OP(
            "sync", "dma_start", out=self.xring[C % 2][:, :],
            in_=self.x_h[C * 128:(C + 1) * 128, :],
            after=[self.xfree[C % 2]] + list(after_extra), inc=("dx", 16))

    def emit_tail(self):
        if self.pending_tail is None:
            return
        C, concat_last, tnw_tok, yl_tok = self.pending_tail
        self.pending_tail = None
        self.pending_ln = None
        OP, tsc = self.OP, self.tsc
        self.kvb_tok[0] = OP("vector", "tensor_copy", self.kvb[:, :], self.kv[:, :],
                             after=[yl_tok], inc="v")
        self.kvb_tok[1] = OP("vector", "tensor_copy", self.kvb2[0:25, :],
                             self.kv2[0:25, :], inc="v")
        tcte = None
        for half in range(2):
            tct = None
            for j in range(8):
                tct = OP("tensor", "transpose", self.pT[:, 128 * j:128 * (j + 1)],
                         self.concat[:, 1024 * half + 128 * j:1024 * half + 128 * (j + 1)],
                         self.idbf[:, :],
                         after=[concat_last, tnw_tok, self.lr["pT"]],
                         inc=("t" if j == 7 else None))
            tcte = OP("scalar", "activation",
                      self.concatT[:, 1024 * half:1024 * (half + 1)], self.pT[:, :],
                      AF.Identity, after=[tct, self.prev_wo], inc="s")
            self.lr["pT"] = tcte
        wo3 = self.wout_s.rearrange("p (s w) -> p s w", s=16)
        two = None
        for half in range(2):
            for kk in range(16):
                two = OP("tensor", "matmul", self.pb01[:, 512 * half:512 * (half + 1)],
                         self.concatT[:, 128 * kk:128 * (kk + 1)],
                         wo3[:, kk, 512 * half:512 * (half + 1)],
                         start=(kk == 0), stop=(kk == 15),
                         after=[tcte, self.tk_wo, self.lr["pb01"]],
                         inc=("t" if (kk == 15 and half == 1) else None))
        self.prev_wo = two
        self.prev_ct = two
        cq = C % 8
        t1sl = self.t1[:, D * cq:D * (cq + 1)]
        tres = OP("vector", "tensor_tensor", t1sl, t1sl, self.pb01[:, :], ALU.add,
                  after=[two, self.t2x_tok[cq]], inc="v")
        self.lr["pb01"] = tres
        self.pending_ln = (cq, tres)

    def emit_ln(self):
        if self.pending_ln is None:
            return
        cq, tres = self.pending_ln
        self.pending_ln = None
        OP = self.OP
        t1sl = self.t1[:, D * cq:D * (cq + 1)]
        tm = OP("scalar", "activation", self.sqscratch[:, :], t1sl, AF.Identity,
                accum_out=self.col(0), after=[tres], inc="s")
        tnm = self.tsc("vector", self.col(1), self.col(0), -1.0 / D, None, ALU.mult,
                       after=[tm], inc="v")
        tv = OP("scalar", "activation", self.sqscratch[:, :], t1sl, AF.Square,
                bias=self.col(1), accum_out=self.col(2), after=[tnm], inc="s")
        tve = self.tsc("vector", self.col(3), self.col(2), 1.0 / D, EPS_LN,
                       ALU.mult, ALU.add, after=[tv], inc="v")
        tl = OP("scalar", "activation", self.col(4), self.col(3), AF.Ln,
                after=[tve], inc="s")
        tr = OP("scalar", "activation", self.col(5), self.col(4), AF.Exp, scale=-0.5,
                after=[tl], inc="s")
        tc = OP("scalar", "activation", self.fscratch[:, :], t1sl, AF.Identity,
                bias=self.col(1), after=[tr], inc="s")
        tf = OP("scalar", "activation", t1sl, self.fscratch[:, :], AF.Identity,
                scale=self.col(5), after=[tc], inc="s")
        self.ln_tok[cq] = tf


    def final_ln_chunk(self, q, j, acc):
        OP = self.OP
        t1sl = self.t1[:, D * j:D * (j + 1)]
        tm = OP("scalar", "activation", self.t1bf[:, :], t1sl, AF.Identity,
                accum_out=self.col(0), after=[acc], inc="s")
        tnm = self.tsc("vector", self.col(1), self.col(0), -1.0 / D, None,
                        ALU.mult, after=[tm], inc="v")
        tv = OP("scalar", "activation", self.t1bf[:, :], t1sl, AF.Square,
                bias=self.col(1), accum_out=self.col(2), after=[tnm], inc="s")
        tve = self.tsc("vector", self.col(3), self.col(2), 1.0 / D, EPS_LN,
                        ALU.mult, ALU.add, after=[tv], inc="v")
        tl = OP("scalar", "activation", self.col(4), self.col(3), AF.Ln,
                after=[tve], inc="s")
        tr = OP("scalar", "activation", self.col(5), self.col(4), AF.Exp,
                scale=-0.5, after=[tl], inc="s")
        tcn = OP("scalar", "activation", self.fscrB[:, :], t1sl, AF.Identity,
                 bias=self.col(1), after=[tr], inc="s")
        tf = OP("scalar", "activation", t1sl, self.fscrB[:, :], AF.Identity,
                scale=self.col(5), after=[tcn], inc="s")
        self.ln_tok[j] = tf
        st = OP("sync", "dma_start",
                out=self.out_h.rearrange("(c p) w -> p c w", p=128)[:, 8 * q + j, :],
                in_=self.t1.rearrange("p (c w) -> p c w", c=8)[:, j, :],
                after=[tf], inc=("do", 16))
        return st

    def phase_B(self, q):
        OP, tsc = self.OP, self.tsc
        pl = self.pl
        self.emit_tail()
        self.emit_ln()
        pl.barrier()
        t1T3 = self.t1T.rearrange("p (s w) -> p s w", s=8)
        act3 = self.act_s.rearrange("p (s w) -> p s w", s=16)
        pw03 = self.pw0.rearrange("p (s w) -> p s w", s=8)
        pw13 = self.pw1.rearrange("p (s w) -> p s w", s=8)
        pw23 = self.pw2.rearrange("p (s w) -> p s w", s=17)

        def load_expert(e, a00, a01, a10, a11, a2, ab):
            t00 = OP("gpsimd", "dma_start", out=pw03[:, :, 0:1024],
                     in_=self.w0t_h[e, :, 0:1024].rearrange("(s p) w -> p s w", p=128),
                     after=[a00], inc=("dw00", 16))
            t10 = OP("gpsimd", "dma_start", out=pw13[:, :, 0:1024],
                     in_=self.w1t_h[e, :, 0:1024].rearrange("(s p) w -> p s w", p=128),
                     after=[a10], inc=("dw10", 16))
            t01 = OP("gpsimd", "dma_start", out=pw03[:, :, 1024:2048],
                     in_=self.w0t_h[e, :, 1024:2048].rearrange("(s p) w -> p s w", p=128),
                     after=[a01], inc=("dw01", 16))
            t11 = OP("gpsimd", "dma_start", out=pw13[:, :, 1024:2048],
                     in_=self.w1t_h[e, :, 1024:2048].rearrange("(s p) w -> p s w", p=128),
                     after=[a11], inc=("dw11", 16))
            t2 = OP("gpsimd", "dma_start", out=pw23,
                    in_=self.w2t_h[e, :, :].rearrange("(s p) w -> p s w", p=128),
                    after=[a2], inc=("dw2", 16))
            tb0 = OP("sync", "dma_start", out=self.b0col[:, :],
                     in_=self.b0r_h[e, :, :], after=[ab], inc=("dbb0", 16))
            tb1 = OP("sync", "dma_start", out=self.b1col[:, :],
                     in_=self.b1r_h[e, :, :], after=[ab], inc=("dbb1", 16))
            return {"w00": t00, "w01": t01, "w10": t10, "w11": t11,
                    "w2": t2, "b0": tb0, "b1": tb1}

        wtok = load_expert(0, None, None, None, None, None, None)

        trt = None
        for j in range(8):
            tcast = tsc("vector", self.t1bf[:, :], self.t1[:, D * j:D * (j + 1)],
                        1.0, None, ALU.mult, after=[self.ln_tok[j]], inc="v")
            ttp = None
            for kk in range(8):
                ttp = OP("tensor", "transpose", self.pT[:, 128 * kk:128 * (kk + 1)],
                         self.t1bf[:, 128 * kk:128 * (kk + 1)], self.idbf[:, :],
                         after=[tcast, self.lr["pT"]],
                         inc=("t" if kk == 7 else None))
            trt = OP("vector", "tensor_copy", t1T3[:, :, 128 * j:128 * (j + 1)],
                     self.pT.rearrange("p (s w) -> p s w", s=8),
                     after=[ttp], inc="v")
            self.lr["pT"] = trt
        gw3 = self.gwt_s.rearrange("p (s w) -> p s w", s=8)
        lg = self.top2[:, 0:8]
        is1 = self.top2[:, 8:16]
        maskd = self.top2[:, 16:24]
        is2 = self.top2[:, 24:32]
        g_tok = None
        for j in range(8):
            tlg = None
            for kk in range(8):
                tlg = OP("tensor", "matmul", self.pb2[:, 0:NE],
                         t1T3[:, kk, 128 * j:128 * (j + 1)], gw3[:, kk, :],
                         start=(kk == 0), stop=(kk == 7),
                         after=[trt, self.tk_gw, self.lr["pb2"]],
                         inc=("t" if kk == 7 else None))
            tlgc = OP("vector", "tensor_tensor", lg, self.pb2[:, 0:NE],
                      self.gbb[:, :], ALU.add, after=[tlg, self.tk_gb], inc="v")
            self.lr["pb2"] = tlgc
            tm1 = tsc("vector", self.scr8[:, :], lg, 1.0, None, ALU.mult, ALU.max,
                      accum=self.col(20), inc="v")
            OP("vector", "tensor_tensor", is1, lg,
               self.col(20).broadcast_to([128, 8]), ALU.is_ge, after=[tm1])
            OP("vector", "scalar_tensor_tensor", maskd, is1, -1e30, lg,
               ALU.mult, ALU.add)
            tm2 = tsc("vector", self.scr8[:, :], maskd, 1.0, None, ALU.mult, ALU.max,
                      accum=self.col(21), inc="v")
            OP("vector", "tensor_tensor", is2, maskd,
               self.col(21).broadcast_to([128, 8]), ALU.is_ge, after=[tm2])
            td = OP("vector", "tensor_tensor", self.col(22), self.col(21),
                    self.col(20), ALU.subtract, inc="v")
            te2 = OP("scalar", "activation", self.col(23), self.col(22), AF.Exp,
                     after=[td], inc="s")
            tdn = tsc("vector", self.col(24), self.col(23), 1.0, None, ALU.add,
                      after=[te2], inc="v")
            tw1 = OP("vector", "reciprocal", self.col(25), self.col(24),
                     after=[tdn], inc="v")
            tw2 = OP("vector", "tensor_tensor", self.col(26), self.col(23),
                     self.col(25), ALU.mult, after=[tw1], inc="v")
            OP("vector", "tensor_tensor", self.G_s[:, 8 * j:8 * (j + 1)], is1,
               self.col(25).broadcast_to([128, 8]), ALU.mult, after=[tw2])
            OP("vector", "tensor_tensor", self.scr8[:, :], is2,
               self.col(26).broadcast_to([128, 8]), ALU.mult)
            g_tok = OP("vector", "tensor_tensor", self.G_s[:, 8 * j:8 * (j + 1)],
                       self.G_s[:, 8 * j:8 * (j + 1)], self.scr8[:, :], ALU.add,
                       inc="v")
            if self.dbg:
                self.dbg_toks.append(OP(
                    "sync", "dma_start",
                    out=self.dg_h[1024 * q + 128 * j:1024 * q + 128 * (j + 1), :],
                    in_=self.G_s[:, 8 * j:8 * (j + 1)], after=[g_tok], inc=("dx", 16)))
        self.dbg_t1_tok = None
        if self.dbg:
            self.dbg_t1_tok = OP(
                "sync", "dma_start",
                out=self.dt1_h.rearrange("(c p) w -> p c w", p=128)[:, 8 * q:8 * (q + 1), :],
                in_=self.t1.rearrange("p (c w) -> p c w", c=8),
                after=[self.ln_tok[7], trt], inc=("dx", 16))
            self.dbg_toks.append(self.dbg_t1_tok)
        self.load_act(self.SET_SILU)

        st_toks = []
        acc_tok = list(self.ln_tok)
        h0_banks = [self.pb2, self.pb6]
        h0_free = [self.lr["pb2"], self.lr["pb6"]]
        h1_free = self.lr["pb3"]
        out_banks = [self.pbA, self.pbB]
        out_free = [self.lr["pb01"], self.lr["pbBlin"]]
        for e in range(NE):
            lastuse = {"w00": None, "w01": None, "w10": None, "w11": None,
                       "w2": None, "b": None}
            for g2 in range(2):
                act_toks = []
                for hs in range(16):
                    ib = hs % 2
                    pbh0 = h0_banks[ib]
                    wh = "0" if hs < 8 else "1"
                    th0 = None
                    for kk in range(8):
                        th0 = OP("tensor", "matmul", pbh0[:, :],
                                 pw03[:, kk, 128 * hs:128 * (hs + 1)],
                                 t1T3[:, kk, 512 * g2:512 * (g2 + 1)],
                                 start=(kk == 0), stop=(kk == 7),
                                 after=[wtok["w0" + wh], trt, h0_free[ib]],
                                 inc=("t" if kk == 7 else None))
                    lastuse["w0" + wh] = th0
                    th1 = None
                    for kk in range(8):
                        th1 = OP("tensor", "matmul", self.pb3[:, :],
                                 pw13[:, kk, 128 * hs:128 * (hs + 1)],
                                 t1T3[:, kk, 512 * g2:512 * (g2 + 1)],
                                 start=(kk == 0), stop=(kk == 7),
                                 after=[wtok["w1" + wh], h1_free],
                                 inc=("t" if kk == 7 else None))
                    lastuse["w1" + wh] = th1
                    tsi = OP("scalar", "activation", self.silu_t[ib][:, :],
                             self.pb3[:, :], AF.Silu,
                             bias=self.b1col[:, hs:hs + 1],
                             after=[th1, wtok["b1"], h0_free[ib]], inc="s")
                    h1_free = tsi
                    thb = OP("scalar", "activation", self.h0b_t[ib][:, :],
                             pbh0[:, :], AF.Identity,
                             bias=self.b0col[:, hs:hs + 1],
                             after=[th0, wtok["b0"], h0_free[ib]], inc="s")
                    lastuse["b"] = thb
                    tac = OP("vector", "tensor_tensor", act3[:, hs, :],
                             self.h0b_t[ib][:, :], self.silu_t[ib][:, :], ALU.mult,
                             after=[tsi, thb, self.prev_w2g], inc="v")
                    h0_free[ib] = tac
                    act_toks.append(tac)
                tw2last = None
                for jj in range(4):
                    j = 4 * g2 + jj
                    ob = (e * 8 + g2 * 4 + jj) % 2
                    pbo = out_banks[ob]
                    to = None
                    for hs in range(16):
                        for nh in range(2):
                            to = OP("tensor", "matmul",
                                    pbo[:, 512 * nh:512 * (nh + 1)],
                                    act3[:, hs, 128 * jj:128 * (jj + 1)],
                                    pw23[:, hs, 512 * nh:512 * (nh + 1)],
                                    start=(hs == 0), stop=False,
                                    after=[act_toks[hs], wtok["w2"], out_free[ob]])
                    for nh in range(2):
                        to = OP("tensor", "matmul",
                                pbo[:, 512 * nh:512 * (nh + 1)],
                                self.r0ones[:, 0:128],
                                pw23[:, 16, 512 * nh:512 * (nh + 1)],
                                start=False, stop=True,
                                after=[self.tk_r0],
                                inc=("t" if nh == 1 else None))
                    tsc2 = OP("scalar", "activation", self.fscrB[:, :], pbo[:, :],
                              AF.Identity,
                              scale=self.G_s[:, 8 * j + e:8 * j + e + 1],
                              after=[to, g_tok, self.prev_tad], inc="s")
                    tad = OP("vector", "tensor_tensor", self.t1[:, D * j:D * (j + 1)],
                             self.t1[:, D * j:D * (j + 1)], self.fscrB[:, :], ALU.add,
                             after=[tsc2, acc_tok[j], self.dbg_t1_tok], inc="v")
                    acc_tok[j] = tad
                    out_free[ob] = tad
                    self.prev_tad = tad
                    tw2last = to
                    if e == NE - 1:
                        if jj == 0:
                            self.load_act(self.SET_NLE)
                        st_toks.append(self.final_ln_chunk(q, j, acc_tok[j]))
                self.prev_w2g = tw2last
                lastuse["w2"] = tw2last
                if e == NE - 1 and g2 == 0:
                    self.load_act(self.SET_SILU)
            if e + 1 < NE:
                wtok = load_expert(e + 1, lastuse["w00"], lastuse["w01"],
                                   lastuse["w10"], lastuse["w11"],
                                   lastuse["w2"], lastuse["b"])
            else:
                last_e = lastuse
        self.lr["pb01"] = out_free[0]
        self.lr["pbBwin"] = out_free[1]
        self.lr["pbBlin"] = out_free[1]
        self.lr["pb2"] = h0_free[0]
        self.lr["pb6"] = h0_free[1]
        self.lr["pb3"] = h1_free
        if q + 1 < self.nq:
            self.tk_wq = self.load_mat(self.wq_s, self.wq_h, 8,
                                       after=[last_e["w01"]])
            self.tk_wk = self.load_mat(self.wk_s, self.wk_h, 8,
                                       after=[last_e["w01"]])
            self.tk_wv = self.load_mat(self.wv_s, self.wv_h, 8,
                                       after=[last_e["w11"]])
            wo3h = self.wout_s.rearrange("p (s w) -> p s w", s=16)
            t_wo1 = self.OP("gpsimd", "dma_start", out=wo3h[:, 0:8, :],
                            in_=self.wout_h[0:1024, :].rearrange("(s p) w -> p s w", p=128),
                            after=[last_e["w11"]], inc=("dw", 16))
            t_wo2 = self.OP("gpsimd", "dma_start", out=wo3h[:, 8:16, :],
                            in_=self.wout_h[1024:2048, :].rearrange("(s p) w -> p s w", p=128),
                            after=[last_e["w2"]], inc=("dw", 16))
            self.tk_wo = t_wo2
        self.store_tok = st_toks[-1]
        self.t1_war = self.store_tok
        if q + 1 < self.nq:
            self.pl.op("sync", "nop", after=[self.store_tok])
            self.pl.barrier()
            OP = self.OP
            OP("vector", "memset", self.qphiT2[:, :], 0.0)
            OP("vector", "memset", self.kphiT2[:, :], 0.0)
            OP("vector", "memset",
               self.v_sup.rearrange("p (h c) -> p h c", c=65)[:, :, 64:65], 1.0,
               inc="v")
            self.prev_tts = [None, None]
            self.prev_kd = None
            self.prev_yl = None
            self.prev_yw = None
            self.prev_ct = None
            self.prev_wo = None
            self.prev_qpad_read = None
            self.prev_phig = None
            self.sup_done = None

    def build(self):
        self.t1_war = None
        self.prev_w2g = None
        self.prev_tad = None
        self.ktail_tok = None
        self.vtail_tok = None
        self.prelude()
        for q in range(self.nq):
            if "A" in self.parts:
                self.superchunk(2 * q)
                if self.nsup is None or self.nsup > 1:
                    self.superchunk(2 * q + 1)
            if "B" in self.parts:
                self.phase_B(q)
            else:
                self.store_tok = self.pl.op(
                    "sync", "dma_start",
                    out=self.out_h.rearrange("(c p) w -> p c w", p=128)[:, 8 * q:8 * (q + 1), :],
                    in_=self.t1.rearrange("p (c w) -> p c w", c=8),
                    after=[tk for tk in self.ln_tok if tk is not None], inc=("do", 16))
        self.pl.op("sync", "nop", after=[("do", self.pl.cnt["do"])])
        if self.dbg:
            self.pl.op("sync", "nop", after=[("dx", self.pl.cnt["dx"])])

        block = self.ctx.enter_context(self.nc.Block())
        pl = self.pl

        def runner(name):
            def _(eng):
                for f in pl.q[name]:
                    f(eng)
            return _
        block.tensor(runner("tensor"))
        block.vector(runner("vector"))
        block.scalar(runner("scalar"))
        block.gpsimd(runner("gpsimd"))
        block.sync(runner("sync"))
        self.ctx.close()
        return self.nc


def build_program(n_quarters=4, dbg=False, parts="AB", nsup=None):
    return Builder(n_quarters=n_quarters, dbg=dbg, parts=parts, nsup=nsup).build()


def _w2t_ext(inputs):
    w2t = np.asarray(inputs["L0_ffn_w2"]).transpose(0, 2, 1)
    ext = np.zeros((NE, 17 * 128, D), dtype=np.float32)
    ext[:, :HID, :] = w2t
    ext[:, HID, :] = np.asarray(inputs["L0_ffn_b2"])
    return np.ascontiguousarray(ext)


def _prep_inputs(inputs):
    f = np.ascontiguousarray
    x = np.asarray(inputs["L0_attn_x"])
    shared = {
        "wq": f(np.asarray(inputs["L0_attn_Wq"])),
        "wk": f(np.asarray(inputs["L0_attn_Wk"])),
        "wv": f(np.asarray(inputs["L0_attn_Wv"])),
        "wqf": f(np.asarray(inputs["L0_attn_Wqf"])),
        "wkf": f(np.asarray(inputs["L0_attn_Wkf"])),
        "wout": f(np.asarray(inputs["L0_attn_Wout"])),
        "normw": f(np.asarray(inputs["L0_attn_norm_w"]).reshape(1, D)),
        "w0t": f(np.asarray(inputs["L0_ffn_w0"]).transpose(0, 2, 1)),
        "w1t": f(np.asarray(inputs["L0_ffn_w1"]).transpose(0, 2, 1)),
        "w2t": _w2t_ext(inputs),
        "b0r": f(np.asarray(inputs["L0_ffn_b0"]).reshape(NE, 16, 128).transpose(0, 2, 1)),
        "b1r": f(np.asarray(inputs["L0_ffn_b1"]).reshape(NE, 16, 128).transpose(0, 2, 1)),
        "gwt": f(np.asarray(inputs["L0_ffn_gate_w"]).T),
        "gb": f(np.asarray(inputs["L0_ffn_gate_b"]).reshape(NE, 1)),
    }
    in_maps = []
    for c in range(8):
        m = dict(shared)
        m["x"] = f(x[c])
        in_maps.append(m)
    return in_maps


def kernel(**inputs):
    in_maps = _prep_inputs(inputs)
    nc = build_program()
    res = run_bass_kernel_spmd(nc, in_maps, core_ids=list(range(8)))
    out = np.stack([np.asarray(r["out"]) for r in res.results], axis=0)
    return out.astype(np.float32)


# version 19
# speedup vs baseline: 1.0652x; 1.0652x over previous
import numpy as np
import ml_dtypes
import concourse.bass as bass
import concourse.mybir as mybir
from concourse.bass_utils import run_bass_kernel_spmd
from concourse.alu_op_type import AluOpType as ALU
from contextlib import ExitStack

F32 = mybir.dt.float32
BF16 = mybir.dt.bfloat16
AF = mybir.ActivationFunctionType

T = 4096
D = 1024
NH = 16
FEAT = 16
NPHI = 153
NCH = 32
NE = 8
HID = 2048
EPS_N = 1e-6
EPS_L = 1e-6
EPS_LN = 1e-5
INV_SQRT2 = float(1.0 / np.sqrt(2.0))
WIN_SCALE = 0.125

ENGS = ("tensor", "vector", "scalar", "gpsimd", "sync")


TSP_METHS = {"tensor_scalar", "scalar_tensor_tensor", "reciprocal"}
DMA_SEMS = {"dx", "dw", "do", "dw00", "dw01", "dw10", "dw11", "dw2", "dbb0", "dbb1"}


class Plan:
    def __init__(self):
        self.q = {e: [] for e in ENGS}
        self.cnt = {}
        self.waited = {e: {} for e in ENGS}
        self.sems = {}
        self.tsp_pending = {e: None for e in ENGS}
        self.esem = {"tensor": "t", "vector": "v", "scalar": "s",
                     "gpsimd": "g", "sync": "dx"}
        self.mclass = None

    def new_sem(self, name, handle):
        self.sems[name] = handle
        self.cnt[name] = 0

    def op(self, eng, meth, *args, after=(), inc=None, **kw):
        after = list(after)
        if eng == "tensor" and meth == "matmul":
            m = args[1].shape[-1] if hasattr(args[1], "shape") else 128
            mc = 32 if m <= 32 else (64 if m <= 64 else 128)
            if self.mclass is not None and mc != self.mclass:
                self.q["tensor"].append(lambda e: e.drain())
            self.mclass = mc
        if meth != "nop" and (eng in ("vector", "scalar")
                              or (eng == "gpsimd" and meth != "dma_start")):
            sn = self.esem[eng]
            if self.cnt[sn] > 0:
                after.append((sn, self.cnt[sn]))
            if inc is None:
                inc = sn
            else:
                nm = inc[0] if isinstance(inc, tuple) else inc
                assert nm == sn, (eng, meth, inc)
        waits = []
        for tk in after:
            if tk is None:
                continue
            s, v = tk
            if s in DMA_SEMS:
                v = self.cnt[s]
            if v > self.waited[eng].get(s, 0):
                waits.append((s, v))
                self.waited[eng][s] = v
        tok = None
        incname = None
        incn = 0
        if inc is not None:
            incname, incn = inc if isinstance(inc, tuple) else (inc, 1)
            self.cnt[incname] += incn
            tok = (incname, self.cnt[incname])

        sems = self.sems

        def f(e, waits=waits, meth=meth, args=args, kw=kw, incname=incname, incn=incn):
            for s, v in waits:
                e.wait_ge(sems[s], v)
            r = getattr(e, meth)(*args, **kw)
            if incname is not None:
                r.then_inc(sems[incname], incn)
        self.q[eng].append(f)
        return tok

    def raw(self, eng, fn, after=()):
        waits = []
        for tk in after:
            if tk is None:
                continue
            s, v = tk
            if s in DMA_SEMS:
                v = self.cnt[s]
            if v > self.waited[eng].get(s, 0):
                waits.append((s, v))
                self.waited[eng][s] = v
        sems = self.sems

        def f(e, waits=waits, fn=fn):
            for s, v in waits:
                e.wait_ge(sems[s], v)
            fn(e)
        self.q[eng].append(f)

    def barrier(self):
        toks = {}
        for e in ENGS:
            own = []
            sn = self.esem.get(e)
            if e in ("vector", "scalar", "gpsimd") and self.cnt.get(sn, 0) > 0:
                own = [(sn, self.cnt[sn])]
            toks[e] = self.op(e, "nop", after=own, inc="b_" + e)
        for e in ENGS:
            self.op(e, "nop", after=[toks[o] for o in ENGS if o != e])


class Builder:
    def __init__(self, n_quarters=4, dbg=False, parts="AB", nsup=None):
        self.nq = n_quarters
        self.dbg = dbg
        self.parts = parts
        self.nsup = nsup
        self.stage = 99
        self.natt = 4
        self.castage = 99
        nc = bass.Bass("TRN2", target_bir_lowering=False)
        self.nc = nc
        self.ctx = ExitStack()
        E = self.ctx.enter_context

        self.x_h = nc.dram_tensor("x", [T, D], F32, kind="ExternalInput").ap()
        self.wq_h = nc.dram_tensor("wq", [D, D], F32, kind="ExternalInput").ap()
        self.wk_h = nc.dram_tensor("wk", [D, D], F32, kind="ExternalInput").ap()
        self.wv_h = nc.dram_tensor("wv", [D, D], F32, kind="ExternalInput").ap()
        self.wqf_h = nc.dram_tensor("wqf", [64, FEAT], F32, kind="ExternalInput").ap()
        self.wkf_h = nc.dram_tensor("wkf", [64, FEAT], F32, kind="ExternalInput").ap()
        self.wout_h = nc.dram_tensor("wout", [2 * D, D], F32, kind="ExternalInput").ap()
        self.normw_h = nc.dram_tensor("normw", [1, D], F32, kind="ExternalInput").ap()
        self.w0t_h = nc.dram_tensor("w0t", [NE, D, HID], F32, kind="ExternalInput").ap()
        self.w1t_h = nc.dram_tensor("w1t", [NE, D, HID], F32, kind="ExternalInput").ap()
        self.w2t_h = nc.dram_tensor("w2t", [NE, 17 * 128, D], F32, kind="ExternalInput").ap()
        self.b0r_h = nc.dram_tensor("b0r", [NE, 128, 16], F32, kind="ExternalInput").ap()
        self.b1r_h = nc.dram_tensor("b1r", [NE, 128, 16], F32, kind="ExternalInput").ap()
        self.gwt_h = nc.dram_tensor("gwt", [D, NE], F32, kind="ExternalInput").ap()
        self.gb_h = nc.dram_tensor("gb", [NE, 1], F32, kind="ExternalInput").ap()
        self.out_h = nc.dram_tensor("out", [T, D], F32, kind="ExternalOutput").ap()
        if dbg:
            self.dt1_h = nc.dram_tensor("d_t1", [T, D], F32, kind="ExternalOutput").ap()
            self.dg_h = nc.dram_tensor("d_g", [T, NE], F32, kind="ExternalOutput").ap()

        self.t1 = E(nc.sbuf_tensor("t1", [128, 8 * D], F32))
        self.kv = E(nc.sbuf_tensor("kv", [128, NH * 65], F32))
        self.kv2 = E(nc.sbuf_tensor("kv2", [32, NH * 65], F32))
        self.kvb = E(nc.sbuf_tensor("kvb", [128, NH * 65], BF16))
        self.kvb2 = E(nc.sbuf_tensor("kvb2", [128, NH * 65], BF16))
        self.ktail = E(nc.sbuf_tensor("ktail", [128, 8 * 128], BF16))
        self.vtail = E(nc.sbuf_tensor("vtail", [128, NH * 65], BF16))
        self.normw_b = E(nc.sbuf_tensor("normw_b", [128, D], BF16))
        self.wqf_s = E(nc.sbuf_tensor("wqf_s", [128, 2 * FEAT], BF16))
        self.wkf_s = E(nc.sbuf_tensor("wkf_s", [128, 2 * FEAT], BF16))
        self.r0ones = E(nc.sbuf_tensor("r0ones", [128, 512], BF16))
        self.maskC = E(nc.sbuf_tensor("maskC", [128, 128], BF16))
        self.maskP = E(nc.sbuf_tensor("maskP", [128, 128], BF16))
        self.maskL = E(nc.sbuf_tensor("maskL", [128, 128], BF16))
        self.idbf = E(nc.sbuf_tensor("idbf", [128, 128], BF16))
        self.gwt_s = E(nc.sbuf_tensor("gwt_s", [128, 8 * NE], BF16))
        self.gbb = E(nc.sbuf_tensor("gbb", [128, NE], F32))
        self.smal = E(nc.sbuf_tensor("smal", [128, 32], F32))
        self.top2 = E(nc.sbuf_tensor("top2", [128, 4 * NE], F32))

        AR_KB = 155
        self.arena = E(nc.sbuf_tensor("arena", [128, AR_KB * 512], BF16))
        ar = self.arena.ap()

        def carve(off_kb, nelem, dt, parts=128):
            offb = int(off_kb * 1024)
            if dt == F32:
                assert offb % 4 == 0
                return ar.bitcast(F32)[0:parts, offb // 4: offb // 4 + nelem]
            return ar[0:parts, offb // 2: offb // 2 + nelem]

        self.wq_s = carve(0, 8 * D, BF16)
        self.wk_s = carve(16, 8 * D, BF16)
        self.wv_s = carve(32, 8 * D, BF16)
        self.wout_s = carve(48, 16 * D, BF16)
        self.hT = carve(80, 8 * 512, BF16)
        self.q_dm = carve(88, 8 * 512, BF16)
        self.k_sup = carve(96, 8 * 512, BF16)
        self.v_sup = carve(104, 4 * NH * 65, BF16)
        self.hbf = carve(112.125, D, BF16)
        self.qf_tm = carve(114.125, 256, F32)
        self.qf_h = carve(115.125, 256, F32)
        self.kf_tm = carve(116.125, 256, F32)
        self.kf_h = carve(117.125, 256, F32)
        self.kphi = carve(118.125, NH * NPHI, BF16)
        self.qphi = carve(122.9375, NH * NPHI, BF16)
        self.qphiT = carve(127.75, 4 * 128, BF16)
        self.qphiT2 = carve(128.75, 4 * 128, BF16)
        self.kphiT = carve(129.75, 4 * 128, BF16)
        self.kphiT2 = carve(130.75, 4 * 128, BF16)
        self.sqA = carve(131.75, 4 * 128, BF16)
        self.eC = carve(132.75, 4 * 128, BF16)
        self.eP = carve(133.75, 4 * 128, BF16)
        self.concat = carve(134.75, 2 * D, BF16)
        self.concatT = self.concat
        self.fscratch = carve(138.75, D, F32)
        self.sqscratch = carve(138.75, D, BF16)
        self.q_pad = carve(142.75, NH * 128, BF16)
        self.nwrow = carve(142.75, D, BF16)
        self.xring = [carve(146.75 + 4 * i, D, F32) for i in range(2)]
        self.pw0 = carve(0, 8 * HID, BF16)
        self.pw1 = carve(32, 8 * HID, BF16)
        self.pw2 = carve(64, 17 * D, BF16)
        self.t1T = carve(98, 8 * D, BF16)
        self.act_s = carve(114, 16 * 512, BF16)
        self.h0b_t = [carve(130 + i, 512, BF16) for i in range(2)]
        self.silu_t = [carve(132 + i, 512, BF16) for i in range(2)]
        self.fscrB = carve(134, D, F32)
        self.t1bf = carve(138, D, BF16)
        self.b0col = carve(140, 16, F32)
        self.b1col = carve(140.0625, 16, F32)
        self.scr8 = carve(140.125, NE, F32)
        self.G_s = carve(140.25, 8 * NE, F32)

        self.pbA = E(nc.psum_tensor("pbA", [128, 1024], F32))
        self.pbB = E(nc.psum_tensor("pbB", [128, 1024], F32))
        self.pb2 = E(nc.psum_tensor("pb2", [128, 512], F32))
        self.pb3 = E(nc.psum_tensor("pb3", [128, 512], F32))
        self.pb6 = E(nc.psum_tensor("pb6", [128, 512], F32))
        self.pT = E(nc.psum_tensor("pT", [128, 1024], BF16))
        self.pb01 = self.pbA

        self.pl = Plan()
        for sn in (["t", "v", "s", "g", "dx", "dw", "do",
                    "dw00", "dw01", "dw10", "dw11", "dw2", "dbb0", "dbb1"]
                   + ["b_" + e for e in ENGS]):
            self.pl.new_sem(sn, E(nc.semaphore("sem_" + sn)))

        from concourse.hw_specs import get_activation_tables
        tabs = list(get_activation_tables(nc.m.arch).values())

        def set_id(need):
            for i, fs in enumerate(tabs):
                if all(f in fs for f in need):
                    return i
            raise ValueError(need)
        self.SET_NLE = set_id([AF.Exp, AF.Ln, AF.Square, AF.Identity])
        self.SET_SILU = set_id([AF.Silu, AF.Identity])

        self.lr = {k: None for k in ["pb01", "pbBwin", "pbBlin", "pb2", "pb3", "pb6", "pT"]}
        self.x_tok = [None] * NCH
        self.xfree = [None, None]
        self.kvb_tok = [None, None]
        self.ln_tok = [None] * 8
        self.t2x_tok = [None] * 8
        self.pending_tail = None
        self.pending_ln = None
        self.tk_wq = self.tk_wk = self.tk_wv = self.tk_wo = None
        self.sup_done = None
        self.ktail_tok = None
        self.vtail_tok = None
        self.prev_tts = None
        self.prev_kd = None
        self.prev_yl = None
        self.prev_yw = None
        self.prev_ct = None
        self.prev_wo = None
        self.prev_qpad_read = None
        self.prev_phig = None
        self.store_tok = None
        self.dbg_toks = []

    def prelude(self):
        OP, tsc, col = self.OP, self.tsc, self.col
        mvals = self.fscratch[0:128, 0:128]
        mvals2 = self.fscratch[0:128, 128:256]
        tki1 = OP("gpsimd", "iota", mvals, pattern=[[1, 128]], base=0,
                  channel_multiplier=-1, allow_small_or_imprecise_dtypes=True, inc="g")
        tki2 = OP("gpsimd", "iota", mvals2, pattern=[[-1, 128]], base=-64,
                  channel_multiplier=1, allow_small_or_imprecise_dtypes=True, inc="g")
        tsc("vector", self.maskL[:, :], mvals, 0.0, None, ALU.is_ge, after=[tki1])
        tsc("vector", self.maskC[:, :], mvals, 64.0, None, ALU.is_le)
        OP("vector", "tensor_tensor", self.maskC[:, :], self.maskC[:, :],
           self.maskL[:, :], ALU.mult)
        tsc("vector", self.maskP[:, :], mvals2[:, :], 0.0, None, ALU.is_ge,
            after=[tki2])
        tkid = tsc("vector", self.idbf[:, :], mvals, 0.0, None, ALU.is_equal, inc="v")
        self.tk_idbf = tkid
        tki3 = OP("gpsimd", "iota", mvals, pattern=[[0, 128]], base=0,
                  channel_multiplier=1, allow_small_or_imprecise_dtypes=True,
                  after=[tkid], inc="g")
        t_r0a = tsc("vector", self.r0ones[:, 0:128], mvals, 0.0, None, ALU.is_equal,
                    after=[tki3], inc="v")
        OP("vector", "tensor_copy", self.r0ones[:, 128:256], self.r0ones[:, 0:128])
        self.tk_r0 = OP("vector", "tensor_copy", self.r0ones[:, 256:512],
                        self.r0ones[:, 0:256], inc="v")
        OP("vector", "memset", self.qphiT2[:, :], 0.0)
        OP("vector", "memset", self.kphiT2[:, :], 0.0)
        OP("vector", "memset",
           self.v_sup.rearrange("p (h c) -> p h c", c=65)[:, :, 64:65], 1.0)
        OP("vector", "memset",
           self.vtail.rearrange("p (h c) -> p h c", c=65)[:, :, 64:65], 1.0)
        OP("vector", "memset", self.t1[:, :], 0.0)
        OP("vector", "memset", self.kv[:, :], 0.0)
        OP("vector", "memset", self.kv2[0:25, :], 0.0)
        self.kvb_tok[0] = OP("vector", "memset", self.kvb[:, :], 0.0, inc="v")
        self.kvb_tok[1] = OP("vector", "memset", self.kvb2[:, :], 0.0, inc="v")

        self.tk_wq = self.load_mat(self.wq_s, self.wq_h, 8)
        self.tk_wk = self.load_mat(self.wk_s, self.wk_h, 8)
        self.tk_wv = self.load_mat(self.wv_s, self.wv_h, 8)
        self.tk_wo = self.load_mat(self.wout_s, self.wout_h, 16)
        t_z1 = OP("vector", "memset", self.wqf_s[:, :], 0.0, inc="v")
        t_z2 = OP("vector", "memset", self.wkf_s[:, :], 0.0, inc="v")
        t1_ = OP("gpsimd", "dma_start", out=self.wqf_s[0:64, 0:FEAT],
                 in_=self.wqf_h[:, :], after=[t_z1], inc=("dw", 16))
        t2_ = OP("gpsimd", "dma_start", out=self.wqf_s[64:128, FEAT:2 * FEAT],
                 in_=self.wqf_h[:, :], inc=("dw", 16))
        self.tk_wqf = t2_
        t3_ = OP("gpsimd", "dma_start", out=self.wkf_s[0:64, 0:FEAT],
                 in_=self.wkf_h[:, :], after=[t_z2], inc=("dw", 16))
        t4_ = OP("gpsimd", "dma_start", out=self.wkf_s[64:128, FEAT:2 * FEAT],
                 in_=self.wkf_h[:, :], inc=("dw", 16))
        self.tk_wkf = t4_
        self.tk_gw = OP("gpsimd", "dma_start",
                        out=self.gwt_s.rearrange("p (s w) -> p s w", s=8),
                        in_=self.gwt_h.rearrange("(s p) w -> p s w", p=128),
                        inc=("dw", 16))
        t_zn = OP("vector", "memset", self.nwrow[:, :], 0.0, inc="v")
        tk_nw = OP("gpsimd", "dma_start", out=self.nwrow[0:1, :],
                   in_=self.normw_h[:, :], after=[t_zn], inc=("dw", 16))
        t_zg = OP("vector", "memset", self.t1bf[:, 0:NE], 0.0, inc="v")
        tk_gbl = OP("gpsimd", "dma_start", out=self.t1bf[0:1, 0:NE],
                    in_=self.gb_h.rearrange("e one -> one e"), after=[t_zg],
                    inc=("dw", 16))
        OP("tensor", "matmul", self.pb01[:, 0:512], self.r0ones[:, 0:128],
           self.nwrow[:, 0:512], start=True, stop=True,
           after=[tk_nw, self.tk_r0, self.kvb_tok[1]])
        tnb = OP("tensor", "matmul", self.pb01[:, 512:1024], self.r0ones[:, 0:128],
                 self.nwrow[:, 512:1024], start=True, stop=True, inc="t")
        self.tk_nbe = OP("vector", "tensor_copy", self.normw_b[:, :], self.pb01[:, :],
                         after=[tnb], inc="v")
        OP("vector", "memset", self.q_pad[:, :], 0.0, after=[tnb])
        tgb = OP("tensor", "matmul", self.pb2[:, 0:NE], self.r0ones[:, 0:128],
                 self.t1bf[:, 0:NE], start=True, stop=True, after=[tk_gbl], inc="t")
        self.tk_gb = OP("vector", "tensor_copy", self.gbb[:, :], self.pb2[:, 0:NE],
                        after=[tgb], inc="v")
        self.lr["pb01"] = self.tk_nbe
        self.lr["pb2"] = self.tk_gb
        self.load_act(self.SET_NLE)
        self.load_x(0)
        self.load_x(1)
        self.pl.barrier()

    def superchunk(self, S):
        OP = self.OP
        hT3 = self.hT.rearrange("p (s w) -> p s w", s=8)
        h_toks = []
        for c in range(4):
            C = 4 * S + c
            xa = self.xring[C % 2]
            t_sq = OP("scalar", "activation", self.sqscratch[:, :], xa[:, :], AF.Square,
                      accum_out=self.col(6), after=[self.x_tok[C], self.sup_done],
                      inc="s")
            t_ms = self.tsc("vector", self.col(7), self.col(6), 1.0 / D, EPS_N,
                            ALU.mult, ALU.add, after=[t_sq], inc="v")
            t_ln = OP("scalar", "activation", self.col(4), self.col(7), AF.Ln,
                      after=[t_ms], inc="s")
            t_rs = OP("scalar", "activation", self.col(5), self.col(4), AF.Exp,
                      scale=-0.5, after=[t_ln], inc="s")
            t_h1 = OP("scalar", "activation", self.hbf[:, :], xa[:, :], AF.Identity,
                      scale=self.col(5), after=[t_rs, self.prev_tts], inc="s")
            cq = C % 8
            t_2x = OP("scalar", "activation", self.t1[:, D * cq:D * (cq + 1)],
                      xa[:, :], AF.Identity, scale=2.0,
                      after=[self.store_tok, self.t1_war], inc="s")
            self.t2x_tok[cq] = t_2x
            self.xfree[C % 2] = t_2x
            self.load_x(C + 2)
            t_h2 = OP("vector", "tensor_tensor", self.hbf[:, :], self.hbf[:, :],
                      self.normw_b[:, :], ALU.mult, after=[t_h1, self.tk_nbe], inc="v")
            tts = None
            for j in range(8):
                tts = OP("tensor", "transpose", self.pT[:, 128 * j:128 * (j + 1)],
                         self.hbf[:, 128 * j:128 * (j + 1)], self.idbf[:, :],
                         after=[t_h2, self.tk_idbf, self.lr["pT"]],
                         inc=("t" if j == 7 else None))
            self.prev_tts = tts
            t_ev = OP("vector", "tensor_copy", hT3[:, :, 128 * c:128 * (c + 1)],
                      self.pT.rearrange("p (s w) -> p s w", s=8),
                      after=[tts, self.sup_done], inc="v")
            self.lr["pT"] = t_ev
            h_toks.append(t_ev)
        h_all = h_toks[-1]
        if self.stage < 2:
            return
        pp = [self.lr["pb01"], self.lr["pb01"]]
        wq3 = self.wq_s.rearrange("p (s w) -> p s w", s=8)
        wk3 = self.wk_s.rearrange("p (s w) -> p s w", s=8)
        wv3 = self.wv_s.rearrange("p (s w) -> p s w", s=8)
        k3 = self.k_sup.rearrange("p (s w) -> p s w", s=8)
        for mj in range(8):
            hf = mj % 2
            tk = None
            for kk in range(8):
                tk = OP("tensor", "matmul", self.pb01[:, 512 * hf:512 * (hf + 1)],
                        wq3[:, kk, 128 * mj:128 * (mj + 1)], hT3[:, kk, :],
                        start=(kk == 0), stop=(kk == 7),
                        after=[self.tk_wq, h_all, pp[hf]],
                        inc=("t" if kk == 7 else None))
            pp[hf] = OP("vector", "tensor_copy", self.q_dm[:, 512 * mj:512 * (mj + 1)],
                        self.pb01[:, 512 * hf:512 * (hf + 1)],
                        after=[tk, self.sup_done], inc="v")
        tk_q = (pp[0], pp[1])
        for mj in range(8):
            hf = mj % 2
            tk = None
            for kk in range(8):
                tk = OP("tensor", "matmul", self.pb01[:, 512 * hf:512 * (hf + 1)],
                        wk3[:, kk, 128 * mj:128 * (mj + 1)], hT3[:, kk, :],
                        start=(kk == 0), stop=(kk == 7),
                        after=[self.tk_wk, pp[hf]],
                        inc=("t" if kk == 7 else None))
            pp[hf] = OP("vector", "tensor_copy", self.k_sup[:, 512 * mj:512 * (mj + 1)],
                        self.pb01[:, 512 * hf:512 * (hf + 1)],
                        after=[tk, self.sup_done], inc="v")
        tk_k = (pp[0], pp[1])
        v65 = self.v_sup.rearrange("p (c h e) -> p c h e", c=4, e=65)
        for c in range(4):
            for nh in range(2):
                hf = nh
                tk = None
                for kk in range(8):
                    tk = OP("tensor", "matmul", self.pb01[:, 512 * hf:512 * (hf + 1)],
                            hT3[:, kk, 128 * c:128 * (c + 1)],
                            wv3[:, kk, 512 * nh:512 * (nh + 1)],
                            start=(kk == 0), stop=(kk == 7),
                            after=[self.tk_wv, pp[hf]],
                            inc=("t" if kk == 7 else None))
                pp[hf] = OP("vector", "tensor_copy",
                            v65[:, c, 8 * nh:8 * (nh + 1), 0:64],
                            self.pb01[:, 512 * hf:512 * (hf + 1)]
                                .rearrange("p (h e) -> p h e", e=64),
                            after=[tk, self.sup_done], inc="v")
        tk_v = (pp[0], pp[1])
        self.lr["pb01"] = pp[1]
        if self.stage < 3:
            return
        for c in range(min(4, self.natt)):
            self.chunk_attn(S, c, 4 * S + c, tk_q, tk_k, tk_v)
        kt = OP("vector", "tensor_copy",
                self.ktail.rearrange("p (s w) -> p s w", s=8),
                k3[:, :, 384:512], inc="v")
        vt = OP("vector", "tensor_copy", self.vtail[:, :],
                self.v_sup[:, 3 * NH * 65:4 * NH * 65], inc="v")
        self.ktail_tok = kt
        self.vtail_tok = vt
        self.sup_done = self.prev_kd

    def chunk_attn(self, S, c, C, tk_q, tk_k, tk_v):
        OP, tsc = self.OP, self.tsc
        vr = self.v_sup[:, c * NH * 65:(c + 1) * NH * 65]
        first_c = (c == 0)
        vprev = self.vtail if first_c else \
            self.v_sup[:, (c - 1) * NH * 65:c * NH * 65]

        def kslice(h_or_sl, width=128, prev=False, whole_slice=True, off=0):
            sl = h_or_sl
            if prev and first_c:
                return self.ktail[:, 128 * sl + off:128 * sl + off + width]
            base = 512 * sl + 128 * (c - 1 if prev else c)
            return self.k_sup[:, base + off:base + off + width]

        qp3 = self.q_pad.rearrange("p (s two w) -> p s two w", s=8, two=2)
        qd3 = self.q_dm.rearrange("p (s w) -> p s w", s=8)
        OP("gpsimd", "tensor_copy", qp3[0:64, :, 0, :],
           qd3[0:64, :, 128 * c:128 * (c + 1)],
           after=[tk_q[0], tk_q[1], self.prev_qpad_read])
        tqp = OP("gpsimd", "tensor_copy", qp3[64:128, :, 1, :],
                 qd3[64:128, :, 128 * c:128 * (c + 1)])
        tk1 = None
        for sl in range(8):
            tk1 = OP("tensor", "matmul", self.pb2[:, 32 * sl:32 * (sl + 1)],
                     self.q_dm[:, 512 * sl + 128 * c:512 * sl + 128 * (c + 1)],
                     self.wqf_s[:, :], start=True, stop=True,
                     after=[tk_q[0], tk_q[1], self.tk_wqf, self.lr["pb2"]],
                     inc=("t" if sl == 7 else None))
        t_qftm = OP("vector", "tensor_copy", self.qf_tm[:, :], self.pb2[:, 0:256],
                    after=[tk1], inc="v")
        tk2 = None
        for sl in range(8):
            tk2 = OP("tensor", "matmul", self.pb2[:, 256 + 32 * sl:256 + 32 * (sl + 1)],
                     kslice(sl), self.wkf_s[:, :], start=True, stop=True,
                     after=[tk_k[0], tk_k[1], self.tk_wkf, t_qftm],
                     inc=("t" if sl == 7 else None))
        t_kftm = OP("vector", "tensor_copy", self.kf_tm[:, :], self.pb2[:, 256:512],
                    after=[tk2], inc="v")
        self.lr["pb2"] = t_kftm
        self.emit_tail()
        if self.castage < 1:
            return
        if self.castage < 2:
            return
        tsc("vector", self.qf_h[:, :], self.qf_tm[:, :], INV_SQRT2, None, ALU.mult,
            after=[self.prev_phig])
        t_sc = tsc("vector", self.kf_h[:, :], self.kf_tm[:, :], INV_SQRT2, None,
                   ALU.mult, inc="v")

        k3d = self.kphi.rearrange("p (h w) -> p h w", h=NH)
        q3d = self.qphi.rearrange("p (h w) -> p h w", h=NH)
        ktm3 = self.kf_tm.rearrange("p (h w) -> p h w", h=NH)
        qtm3 = self.qf_tm.rearrange("p (h w) -> p h w", h=NH)
        khh3 = self.kf_h.rearrange("p (h w) -> p h w", h=NH)
        qhh3 = self.qf_h.rearrange("p (h w) -> p h w", h=NH)
        OP("vector", "memset", k3d[:, :, 0:1], 1.0, after=[self.prev_kd])
        OP("vector", "memset", q3d[:, :, 0:1], 1.0)
        OP("vector", "tensor_copy", k3d[:, :, 1:17], ktm3[:, :, :])
        OP("vector", "tensor_copy", q3d[:, :, 1:17], qtm3[:, :, :])
        def _two_sided(ap_k, ap_q):
            d = ap_q.offset - ap_k.offset
            return bass.AP(tensor=ap_k.tensor, offset=ap_k.offset,
                           ap=[list(p) for p in ap_k.ap[:1]] +
                              [[d, 2]] + [list(p) for p in ap_k.ap[1:]])
        off = 17
        for i in range(FEAT):
            n = FEAT - i
            if n > 1:
                outv = _two_sided(k3d[:, :, off + 1:off + n],
                                  q3d[:, :, off + 1:off + n])
                in0v = _two_sided(
                    khh3[:, :, i:i + 1].broadcast_to([128, NH, n - 1]),
                    qhh3[:, :, i:i + 1].broadcast_to([128, NH, n - 1]))
                in1v = _two_sided(ktm3[:, :, i + 1:FEAT], qtm3[:, :, i + 1:FEAT])
                OP("vector", "tensor_tensor", outv, in0v, in1v, ALU.mult)
            off += n
        off = 17
        t_phiv = None
        for i in range(FEAT):
            n = FEAT - i
            outd = _two_sided(k3d[:, :, off:off + 1], q3d[:, :, off:off + 1])
            ind = _two_sided(khh3[:, :, i:i + 1], qhh3[:, :, i:i + 1])
            t_phiv = OP("vector", "tensor_tensor", outd, ind, ind, ALU.mult,
                        inc=("v" if i == FEAT - 1 else None))
            off += n
        t_phig = t_phiv
        self.prev_phig = t_phig
        t_phi = t_phiv
        def ywin_bank(hg):
            return (self.pb2, 0, "pb2") if hg % 2 == 0 else (self.pbB, 512, "pbBwin")

        def emit_win_norm(hg, tyw):
            pb, base, key = ywin_bank(hg)
            zwin = pb[:, base:base + 260].rearrange(
                "p (j w) -> p j w", w=65)[:, :, 64:65]
            tlz = OP("scalar", "activation", self.smal[:, 20:24],
                     zwin.squeeze(2), AF.Ln, after=[tyw], inc="s")
            rcw = OP("scalar", "activation", self.smal[:, 16:20],
                     self.smal[:, 20:24], AF.Exp, scale=-1.0, inc="s")
            c3 = self.concat.rearrange("p (half h w) -> p half h w", half=2, w=64)
            tnw = None
            for j in range(4):
                tnw = OP("scalar", "activation", c3[:, 1, 4 * hg + j, :],
                         pb[:, base + 65 * j:base + 65 * j + 64],
                         AF.Identity, scale=self.smal[:, 16 + j:17 + j],
                         after=[rcw, self.prev_ct], inc="s")
            self.lr[key] = tnw
            return tnw
        win_toks = []
        pend_norm = None
        for hg in range(4):
            hs4 = [4 * hg + j for j in range(4)]
            tsC = None
            for j, h in enumerate(hs4):
                tsC = OP("tensor", "matmul", self.pb3[:, 128 * j:128 * (j + 1)],
                         kslice(h // 2), self.q_pad[:, 128 * h:128 * (h + 1)],
                         start=True, stop=True,
                         after=[tqp, tk_k[0], tk_k[1], self.lr["pb3"]],
                         inc=("t" if j == 3 else None))
            teC = OP("scalar", "activation", self.eC[:, :], self.pb3[:, :],
                     AF.Exp, scale=WIN_SCALE, after=[tsC, self.prev_yw], inc="s")
            teCm = OP("gpsimd", "tensor_tensor",
                      self.eC.rearrange("p (j w) -> p j w", j=4),
                      self.eC.rearrange("p (j w) -> p j w", j=4),
                      self.maskC[:, :].unsqueeze(1).broadcast_to([128, 4, 128]),
                      ALU.mult, after=[teC])
            self.lr["pb3"] = teCm
            tePm = None
            if C > 0:
                tsP = None
                for j, h in enumerate(hs4):
                    tsP = OP("tensor", "matmul",
                             self.pb6[:, 128 * j:128 * (j + 1)],
                             kslice(h // 2, prev=True),
                             self.q_pad[:, 128 * h:128 * (h + 1)],
                             start=True, stop=True,
                             after=[self.lr["pb6"],
                                    self.ktail_tok if first_c else None],
                             inc=("t" if j == 3 else None))
                teP = OP("scalar", "activation", self.eP[:, :], self.pb6[:, :],
                         AF.Exp, scale=WIN_SCALE, after=[tsP, self.prev_yw], inc="s")
                tePm = OP("gpsimd", "tensor_tensor",
                          self.eP.rearrange("p (j w) -> p j w", j=4),
                          self.eP.rearrange("p (j w) -> p j w", j=4),
                          self.maskP[:, :].unsqueeze(1).broadcast_to([128, 4, 128]),
                          ALU.mult, after=[teP])
                self.lr["pb6"] = tePm
            self.prev_qpad_read = tsP if C > 0 else tsC
            tyw = None
            ywpb, ywbase, ywkey = ywin_bank(hg)
            for j, h in enumerate(hs4):
                tk = OP("tensor", "matmul",
                        ywpb[:, ywbase + 65 * j:ywbase + 65 * (j + 1)],
                        self.eC[:, 128 * j:128 * (j + 1)], vr[:, 65 * h:65 * (h + 1)],
                        start=True, stop=(C == 0),
                        after=[teCm, tk_v[0], tk_v[1], self.lr[ywkey]],
                        inc=("t" if (C == 0 and j == 3) else None))
                if C > 0:
                    tk = OP("tensor", "matmul",
                            ywpb[:, ywbase + 65 * j:ywbase + 65 * (j + 1)],
                            self.eP[:, 128 * j:128 * (j + 1)],
                            vprev[:, 65 * h:65 * (h + 1)],
                            start=False, stop=True,
                            after=[tePm, self.vtail_tok if first_c else None],
                            inc=("t" if j == 3 else None))
                tyw = tk
            self.prev_yw = tyw
            if pend_norm is not None:
                win_toks.append(emit_win_norm(*pend_norm))
            pend_norm = (hg, tyw)
        win_toks.append(emit_win_norm(*pend_norm))
        if self.castage < 5:
            return
        concat_last = None
        t_kd2 = None
        for hg in range(4):
            hs4 = [4 * hg + j for j in range(4)]
            for phi_src, dT, dT2 in ((self.qphi, self.qphiT, self.qphiT2),
                                     (self.kphi, self.kphiT, self.kphiT2)):
                ttr = None
                for j, h in enumerate(hs4):
                    ttr = OP("tensor", "transpose", self.pT[:, 128 * j:128 * (j + 1)],
                             phi_src[:, NPHI * h:NPHI * h + 128], self.idbf[:, :],
                             after=[t_phi, t_phig, self.lr["pT"], self.prev_yl])
                for j, h in enumerate(hs4):
                    ttr = OP("tensor", "transpose",
                             self.pT[0:25, 512 + 128 * j:512 + 128 * (j + 1)],
                             phi_src[:, NPHI * h + 128:NPHI * h + 153], self.idbf[:, :],
                             inc=("t" if j == 3 else None))
                OP("scalar", "activation", dT[:, :], self.pT[:, 0:512], AF.Identity,
                   after=[ttr, self.prev_yl])
                t_pe = OP("scalar", "activation", dT2[0:25, :],
                          self.pT[0:25, 512:1024], AF.Identity, inc="s")
                self.lr["pT"] = t_pe
            tqk = None
            for j in range(4):
                OP("tensor", "matmul", self.pb3[:, 128 * j:128 * (j + 1)],
                   self.kphiT[:, 128 * j:128 * (j + 1)],
                   self.qphiT[:, 128 * j:128 * (j + 1)],
                   start=True, stop=False, after=[t_pe, self.lr["pb3"]])
                tqk = OP("tensor", "matmul", self.pb3[:, 128 * j:128 * (j + 1)],
                         self.kphiT2[:, 128 * j:128 * (j + 1)],
                         self.qphiT2[:, 128 * j:128 * (j + 1)],
                         start=False, stop=True,
                         inc=("t" if j == 3 else None))
            tAe = OP("scalar", "activation", self.sqA[:, :], self.pb3[:, :],
                     AF.Identity, after=[tqk, self.prev_yl], inc="s")
            tAm = OP("gpsimd", "tensor_tensor",
                     self.sqA.rearrange("p (j w) -> p j w", j=4),
                     self.sqA.rearrange("p (j w) -> p j w", j=4),
                     self.maskL[:, :].unsqueeze(1).broadcast_to([128, 4, 128]),
                     ALU.mult, after=[tAe])
            self.lr["pb3"] = tAe
            tyl = None
            for j, h in enumerate(hs4):
                OP("tensor", "matmul", self.pbB[:, 65 * j:65 * (j + 1)],
                   self.sqA[:, 128 * j:128 * (j + 1)], vr[:, 65 * h:65 * (h + 1)],
                   start=True, stop=False,
                   after=[tAm, self.lr["pbBlin"]])
                OP("tensor", "matmul", self.pbB[:, 65 * j:65 * (j + 1)],
                   self.qphiT[:, 128 * j:128 * (j + 1)],
                   self.kvb[:, 65 * h:65 * (h + 1)],
                   start=False, stop=False, after=[self.kvb_tok[0]])
                tyl = OP("tensor", "matmul", self.pbB[:, 65 * j:65 * (j + 1)],
                         self.qphiT2[:, 128 * j:128 * (j + 1)],
                         self.kvb2[:, 65 * h:65 * (h + 1)],
                         start=False, stop=True, after=[self.kvb_tok[1]],
                         inc=("t" if j == 3 else None))
            self.prev_yl = tyl
            if self.castage < 8:
                continue
            t_kd = None
            for j, h in enumerate(hs4):
                t_kd = OP("tensor", "matmul", self.pb6[:, 65 * j:65 * (j + 1)],
                          self.kphi[:, NPHI * h:NPHI * h + 128],
                          vr[:, 65 * h:65 * (h + 1)],
                          start=True, stop=True,
                          after=[t_phi, t_phig, tyl, self.lr["pb6"]],
                          inc=("t" if j == 3 else None))
            t_kd2 = None
            for j, h in enumerate(hs4):
                t_kd2 = OP("tensor", "matmul", self.pb3[0:25, 65 * j:65 * (j + 1)],
                           self.kphi[:, NPHI * h + 128:NPHI * h + 153],
                           vr[:, 65 * h:65 * (h + 1)],
                           start=True, stop=True, after=[self.lr["pb3"]],
                           inc=("t" if j == 3 else None))
            if self.castage < 9:
                continue
            zlin = self.pbB[:, 0:260].rearrange("p (j w) -> p j w", w=65)[:, :, 64:65]
            tsc("vector", self.smal[:, 8:12], zlin.squeeze(2), 1.0, EPS_L,
                ALU.mult, ALU.add, after=[tyl])
            rc = OP("vector", "reciprocal", self.smal[:, 12:16], self.smal[:, 8:12],
                    inc="v")
            c3 = self.concat.rearrange("p (half h w) -> p half h w", half=2, w=64)
            tnl = OP("vector", "tensor_tensor", c3[:, 0, 4 * hg:4 * hg + 4, :],
                     self.pbB[:, 0:260].rearrange("p (j w) -> p j w", w=65)[:, :, 0:64],
                     self.smal[:, 12:16].unsqueeze(2).broadcast_to([128, 4, 64]),
                     ALU.mult, after=[rc, self.prev_ct], inc="v")
            self.lr["pbBlin"] = tnl
            concat_last = tnl
            kv3 = self.kv.rearrange("p (h e) -> p h e", e=65)
            kv23 = self.kv2.rearrange("p (h e) -> p h e", e=65)
            OP("vector", "tensor_tensor", kv3[:, 4 * hg:4 * hg + 4, :],
               kv3[:, 4 * hg:4 * hg + 4, :],
               self.pb6[:, 0:260].rearrange("p (j w) -> p j w", w=65),
               ALU.add, after=[t_kd])
            tkva = OP("vector", "tensor_tensor", kv23[0:25, 4 * hg:4 * hg + 4, :],
                      kv23[0:25, 4 * hg:4 * hg + 4, :],
                      self.pb3[0:25, 0:260].rearrange("p (j w) -> p j w", w=65),
                      ALU.add, after=[t_kd2], inc="v")
            self.lr["pb6"] = tkva
            self.lr["pb3"] = tkva
        if self.castage < 10:
            return
        self.prev_kd = t_kd2
        self.emit_ln()
        self.pending_tail = (C, concat_last, win_toks[-1], self.prev_yl)

    def OP(self, *a, **k):
        return self.pl.op(*a, **k)

    def tsc(self, eng, out, in0, s1, s2, op0, op1=None, after=(), inc=None, accum=None):
        if accum is not None:
            return self.OP(eng, "tensor_scalar", out, in0, s1, None, op0, op1,
                           accum_out=accum, after=after, inc=inc)
        if op1 is None:
            return self.OP(eng, "tensor_scalar", out, in0, s1, None, op0,
                           after=after, inc=inc)
        return self.OP(eng, "tensor_scalar", out, in0, s1, s2, op0, op1,
                       after=after, inc=inc)

    def col(self, i):
        return self.smal[:, i:i + 1]

    def load_act(self, set_id, after=()):
        nc = self.nc

        def fn(e):
            e.add_instruction(mybir.InstLoadActFuncSet(
                name=nc.get_next_instruction_name(),
                act_func_set_id=set_id, ins=[], outs=[]))
        self.pl.raw("scalar", fn, after=after)

    def load_mat(self, dst, src, nslice, after=()):
        return self.OP("gpsimd", "dma_start",
                       out=dst.rearrange("p (s w) -> p s w", s=nslice),
                       in_=src.rearrange("(s p) w -> p s w", p=128),
                       after=after, inc=("dw", 16))

    def load_x(self, C, after_extra=()):
        if C >= self.nq * 8:
            return
        self.x_tok[C] = self.OP(
            "sync", "dma_start", out=self.xring[C % 2][:, :],
            in_=self.x_h[C * 128:(C + 1) * 128, :],
            after=[self.xfree[C % 2]] + list(after_extra), inc=("dx", 16))

    def emit_tail(self):
        if self.pending_tail is None:
            return
        C, concat_last, tnw_tok, yl_tok = self.pending_tail
        self.pending_tail = None
        self.pending_ln = None
        OP, tsc = self.OP, self.tsc
        self.kvb_tok[0] = OP("vector", "tensor_copy", self.kvb[:, :], self.kv[:, :],
                             after=[yl_tok], inc="v")
        self.kvb_tok[1] = OP("vector", "tensor_copy", self.kvb2[0:25, :],
                             self.kv2[0:25, :], inc="v")
        tcte = None
        for half in range(2):
            tct = None
            for j in range(8):
                tct = OP("tensor", "transpose", self.pT[:, 128 * j:128 * (j + 1)],
                         self.concat[:, 1024 * half + 128 * j:1024 * half + 128 * (j + 1)],
                         self.idbf[:, :],
                         after=[concat_last, tnw_tok, self.lr["pT"]],
                         inc=("t" if j == 7 else None))
            tcte = OP("scalar", "activation",
                      self.concatT[:, 1024 * half:1024 * (half + 1)], self.pT[:, :],
                      AF.Identity, after=[tct, self.prev_wo], inc="s")
            self.lr["pT"] = tcte
        wo3 = self.wout_s.rearrange("p (s w) -> p s w", s=16)
        two = None
        for half in range(2):
            for kk in range(16):
                two = OP("tensor", "matmul", self.pb01[:, 512 * half:512 * (half + 1)],
                         self.concatT[:, 128 * kk:128 * (kk + 1)],
                         wo3[:, kk, 512 * half:512 * (half + 1)],
                         start=(kk == 0), stop=(kk == 15),
                         after=[tcte, self.tk_wo, self.lr["pb01"]],
                         inc=("t" if (kk == 15 and half == 1) else None))
        self.prev_wo = two
        self.prev_ct = two
        cq = C % 8
        t1sl = self.t1[:, D * cq:D * (cq + 1)]
        tres = OP("vector", "tensor_tensor", t1sl, t1sl, self.pb01[:, :], ALU.add,
                  after=[two, self.t2x_tok[cq]], inc="v")
        self.lr["pb01"] = tres
        self.pending_ln = (cq, tres)

    def emit_ln(self):
        if self.pending_ln is None:
            return
        cq, tres = self.pending_ln
        self.pending_ln = None
        OP = self.OP
        t1sl = self.t1[:, D * cq:D * (cq + 1)]
        tm = OP("scalar", "activation", self.sqscratch[:, :], t1sl, AF.Identity,
                accum_out=self.col(0), after=[tres], inc="s")
        tnm = self.tsc("vector", self.col(1), self.col(0), -1.0 / D, None, ALU.mult,
                       after=[tm], inc="v")
        tv = OP("scalar", "activation", self.sqscratch[:, :], t1sl, AF.Square,
                bias=self.col(1), accum_out=self.col(2), after=[tnm], inc="s")
        tve = self.tsc("vector", self.col(3), self.col(2), 1.0 / D, EPS_LN,
                       ALU.mult, ALU.add, after=[tv], inc="v")
        tl = OP("scalar", "activation", self.col(4), self.col(3), AF.Ln,
                after=[tve], inc="s")
        tr = OP("scalar", "activation", self.col(5), self.col(4), AF.Exp, scale=-0.5,
                after=[tl], inc="s")
        tc = OP("scalar", "activation", self.fscratch[:, :], t1sl, AF.Identity,
                bias=self.col(1), after=[tr], inc="s")
        tf = OP("scalar", "activation", t1sl, self.fscratch[:, :], AF.Identity,
                scale=self.col(5), after=[tc], inc="s")
        self.ln_tok[cq] = tf


    def final_ln_chunk(self, q, j, acc):
        OP = self.OP
        t1sl = self.t1[:, D * j:D * (j + 1)]
        tm = OP("scalar", "activation", self.t1bf[:, :], t1sl, AF.Identity,
                accum_out=self.col(0), after=[acc], inc="s")
        tnm = self.tsc("vector", self.col(1), self.col(0), -1.0 / D, None,
                        ALU.mult, after=[tm], inc="v")
        tv = OP("scalar", "activation", self.t1bf[:, :], t1sl, AF.Square,
                bias=self.col(1), accum_out=self.col(2), after=[tnm], inc="s")
        tve = self.tsc("vector", self.col(3), self.col(2), 1.0 / D, EPS_LN,
                        ALU.mult, ALU.add, after=[tv], inc="v")
        tl = OP("scalar", "activation", self.col(4), self.col(3), AF.Ln,
                after=[tve], inc="s")
        tr = OP("scalar", "activation", self.col(5), self.col(4), AF.Exp,
                scale=-0.5, after=[tl], inc="s")
        tcn = OP("scalar", "activation", self.fscrB[:, :], t1sl, AF.Identity,
                 bias=self.col(1), after=[tr], inc="s")
        tf = OP("scalar", "activation", t1sl, self.fscrB[:, :], AF.Identity,
                scale=self.col(5), after=[tcn], inc="s")
        self.ln_tok[j] = tf
        st = OP("sync", "dma_start",
                out=self.out_h.rearrange("(c p) w -> p c w", p=128)[:, 8 * q + j, :],
                in_=self.t1.rearrange("p (c w) -> p c w", c=8)[:, j, :],
                after=[tf], inc=("do", 16))
        return st

    def phase_B(self, q):
        OP, tsc = self.OP, self.tsc
        pl = self.pl
        self.emit_tail()
        self.emit_ln()
        pl.barrier()
        t1T3 = self.t1T.rearrange("p (s w) -> p s w", s=8)
        act3 = self.act_s.rearrange("p (s w) -> p s w", s=16)
        pw03 = self.pw0.rearrange("p (s w) -> p s w", s=8)
        pw13 = self.pw1.rearrange("p (s w) -> p s w", s=8)
        pw23 = self.pw2.rearrange("p (s w) -> p s w", s=17)

        def load_expert(e, a00, a01, a10, a11, a2, ab):
            t00 = OP("gpsimd", "dma_start", out=pw03[:, :, 0:1024],
                     in_=self.w0t_h[e, :, 0:1024].rearrange("(s p) w -> p s w", p=128),
                     after=[a00], inc=("dw00", 16))
            t10 = OP("gpsimd", "dma_start", out=pw13[:, :, 0:1024],
                     in_=self.w1t_h[e, :, 0:1024].rearrange("(s p) w -> p s w", p=128),
                     after=[a10], inc=("dw10", 16))
            t01 = OP("gpsimd", "dma_start", out=pw03[:, :, 1024:2048],
                     in_=self.w0t_h[e, :, 1024:2048].rearrange("(s p) w -> p s w", p=128),
                     after=[a01], inc=("dw01", 16))
            t11 = OP("gpsimd", "dma_start", out=pw13[:, :, 1024:2048],
                     in_=self.w1t_h[e, :, 1024:2048].rearrange("(s p) w -> p s w", p=128),
                     after=[a11], inc=("dw11", 16))
            t2 = OP("gpsimd", "dma_start", out=pw23,
                    in_=self.w2t_h[e, :, :].rearrange("(s p) w -> p s w", p=128),
                    after=[a2], inc=("dw2", 16))
            tb0 = OP("sync", "dma_start", out=self.b0col[:, :],
                     in_=self.b0r_h[e, :, :], after=[ab], inc=("dbb0", 16))
            tb1 = OP("sync", "dma_start", out=self.b1col[:, :],
                     in_=self.b1r_h[e, :, :], after=[ab], inc=("dbb1", 16))
            return {"w00": t00, "w01": t01, "w10": t10, "w11": t11,
                    "w2": t2, "b0": tb0, "b1": tb1}

        wtok = load_expert(0, None, None, None, None, None, None)

        trt = None
        for j in range(8):
            tcast = tsc("vector", self.t1bf[:, :], self.t1[:, D * j:D * (j + 1)],
                        1.0, None, ALU.mult, after=[self.ln_tok[j]], inc="v")
            ttp = None
            for kk in range(8):
                ttp = OP("tensor", "transpose", self.pT[:, 128 * kk:128 * (kk + 1)],
                         self.t1bf[:, 128 * kk:128 * (kk + 1)], self.idbf[:, :],
                         after=[tcast, self.lr["pT"]],
                         inc=("t" if kk == 7 else None))
            trt = OP("vector", "tensor_copy", t1T3[:, :, 128 * j:128 * (j + 1)],
                     self.pT.rearrange("p (s w) -> p s w", s=8),
                     after=[ttp], inc="v")
            self.lr["pT"] = trt
        gw3 = self.gwt_s.rearrange("p (s w) -> p s w", s=8)
        lg = self.top2[:, 0:8]
        is1 = self.top2[:, 8:16]
        maskd = self.top2[:, 16:24]
        is2 = self.top2[:, 24:32]
        g_tok = None
        for j in range(8):
            tlg = None
            for kk in range(8):
                tlg = OP("tensor", "matmul", self.pb2[:, 0:NE],
                         t1T3[:, kk, 128 * j:128 * (j + 1)], gw3[:, kk, :],
                         start=(kk == 0), stop=(kk == 7),
                         after=[trt, self.tk_gw, self.lr["pb2"]],
                         inc=("t" if kk == 7 else None))
            tlgc = OP("vector", "tensor_tensor", lg, self.pb2[:, 0:NE],
                      self.gbb[:, :], ALU.add, after=[tlg, self.tk_gb], inc="v")
            self.lr["pb2"] = tlgc
            tm1 = tsc("vector", self.scr8[:, :], lg, 1.0, None, ALU.mult, ALU.max,
                      accum=self.col(20), inc="v")
            OP("vector", "tensor_tensor", is1, lg,
               self.col(20).broadcast_to([128, 8]), ALU.is_ge, after=[tm1])
            OP("vector", "scalar_tensor_tensor", maskd, is1, -1e30, lg,
               ALU.mult, ALU.add)
            tm2 = tsc("vector", self.scr8[:, :], maskd, 1.0, None, ALU.mult, ALU.max,
                      accum=self.col(21), inc="v")
            OP("vector", "tensor_tensor", is2, maskd,
               self.col(21).broadcast_to([128, 8]), ALU.is_ge, after=[tm2])
            td = OP("vector", "tensor_tensor", self.col(22), self.col(21),
                    self.col(20), ALU.subtract, inc="v")
            te2 = OP("scalar", "activation", self.col(23), self.col(22), AF.Exp,
                     after=[td], inc="s")
            tdn = tsc("vector", self.col(24), self.col(23), 1.0, None, ALU.add,
                      after=[te2], inc="v")
            tw1 = OP("vector", "reciprocal", self.col(25), self.col(24),
                     after=[tdn], inc="v")
            tw2 = OP("vector", "tensor_tensor", self.col(26), self.col(23),
                     self.col(25), ALU.mult, after=[tw1], inc="v")
            OP("vector", "tensor_tensor", self.G_s[:, 8 * j:8 * (j + 1)], is1,
               self.col(25).broadcast_to([128, 8]), ALU.mult, after=[tw2])
            OP("vector", "tensor_tensor", self.scr8[:, :], is2,
               self.col(26).broadcast_to([128, 8]), ALU.mult)
            g_tok = OP("vector", "tensor_tensor", self.G_s[:, 8 * j:8 * (j + 1)],
                       self.G_s[:, 8 * j:8 * (j + 1)], self.scr8[:, :], ALU.add,
                       inc="v")
            if self.dbg:
                self.dbg_toks.append(OP(
                    "sync", "dma_start",
                    out=self.dg_h[1024 * q + 128 * j:1024 * q + 128 * (j + 1), :],
                    in_=self.G_s[:, 8 * j:8 * (j + 1)], after=[g_tok], inc=("dx", 16)))
        self.dbg_t1_tok = None
        if self.dbg:
            self.dbg_t1_tok = OP(
                "sync", "dma_start",
                out=self.dt1_h.rearrange("(c p) w -> p c w", p=128)[:, 8 * q:8 * (q + 1), :],
                in_=self.t1.rearrange("p (c w) -> p c w", c=8),
                after=[self.ln_tok[7], trt], inc=("dx", 16))
            self.dbg_toks.append(self.dbg_t1_tok)
        self.load_act(self.SET_SILU)

        st_toks = []
        acc_tok = list(self.ln_tok)
        h0_banks = [self.pb2, self.pb6]
        h0_free = [self.lr["pb2"], self.lr["pb6"]]
        h1_free = self.lr["pb3"]
        out_banks = [self.pbA, self.pbB]
        out_free = [self.lr["pb01"], self.lr["pbBlin"]]
        for e in range(NE):
            lastuse = {"w00": None, "w01": None, "w10": None, "w11": None,
                       "w2": None, "b": None}
            for g2 in range(2):
                act_toks = []
                for hs in range(16):
                    ib = hs % 2
                    pbh0 = h0_banks[ib]
                    wh = "0" if hs < 8 else "1"
                    th0 = None
                    for kk in range(8):
                        th0 = OP("tensor", "matmul", pbh0[:, :],
                                 pw03[:, kk, 128 * hs:128 * (hs + 1)],
                                 t1T3[:, kk, 512 * g2:512 * (g2 + 1)],
                                 start=(kk == 0), stop=(kk == 7),
                                 after=[wtok["w0" + wh], trt, h0_free[ib]],
                                 inc=("t" if kk == 7 else None))
                    lastuse["w0" + wh] = th0
                    th1 = None
                    for kk in range(8):
                        th1 = OP("tensor", "matmul", self.pb3[:, :],
                                 pw13[:, kk, 128 * hs:128 * (hs + 1)],
                                 t1T3[:, kk, 512 * g2:512 * (g2 + 1)],
                                 start=(kk == 0), stop=(kk == 7),
                                 after=[wtok["w1" + wh], h1_free],
                                 inc=("t" if kk == 7 else None))
                    lastuse["w1" + wh] = th1
                    tsi = OP("scalar", "activation", self.silu_t[ib][:, :],
                             self.pb3[:, :], AF.Silu,
                             bias=self.b1col[:, hs:hs + 1],
                             after=[th1, wtok["b1"], h0_free[ib]], inc="s")
                    h1_free = tsi
                    thb = OP("scalar", "activation", self.h0b_t[ib][:, :],
                             pbh0[:, :], AF.Identity,
                             bias=self.b0col[:, hs:hs + 1],
                             after=[th0, wtok["b0"], h0_free[ib]], inc="s")
                    lastuse["b"] = thb
                    tac = OP("vector", "tensor_tensor", act3[:, hs, :],
                             self.h0b_t[ib][:, :], self.silu_t[ib][:, :], ALU.mult,
                             after=[tsi, thb, self.prev_w2g], inc="v")
                    h0_free[ib] = tac
                    act_toks.append(tac)
                tw2last = None
                for jj in range(4):
                    j = 4 * g2 + jj
                    ob = (e * 8 + g2 * 4 + jj) % 2
                    pbo = out_banks[ob]
                    to = None
                    for hs in range(16):
                        for nh in range(2):
                            to = OP("tensor", "matmul",
                                    pbo[:, 512 * nh:512 * (nh + 1)],
                                    act3[:, hs, 128 * jj:128 * (jj + 1)],
                                    pw23[:, hs, 512 * nh:512 * (nh + 1)],
                                    start=(hs == 0), stop=False,
                                    after=[act_toks[hs], wtok["w2"], out_free[ob]])
                    for nh in range(2):
                        to = OP("tensor", "matmul",
                                pbo[:, 512 * nh:512 * (nh + 1)],
                                self.r0ones[:, 0:128],
                                pw23[:, 16, 512 * nh:512 * (nh + 1)],
                                start=False, stop=True,
                                after=[self.tk_r0],
                                inc=("t" if nh == 1 else None))
                    tsc2 = OP("scalar", "activation", self.fscrB[:, :], pbo[:, :],
                              AF.Identity,
                              scale=self.G_s[:, 8 * j + e:8 * j + e + 1],
                              after=[to, g_tok, self.prev_tad], inc="s")
                    tad = OP("vector", "tensor_tensor", self.t1[:, D * j:D * (j + 1)],
                             self.t1[:, D * j:D * (j + 1)], self.fscrB[:, :], ALU.add,
                             after=[tsc2, acc_tok[j], self.dbg_t1_tok], inc="v")
                    acc_tok[j] = tad
                    out_free[ob] = tad
                    self.prev_tad = tad
                    tw2last = to
                    if e == NE - 1:
                        if jj == 0:
                            self.load_act(self.SET_NLE)
                        st_toks.append(self.final_ln_chunk(q, j, acc_tok[j]))
                self.prev_w2g = tw2last
                lastuse["w2"] = tw2last
                if e == NE - 1 and g2 == 0:
                    self.load_act(self.SET_SILU)
            if e + 1 < NE:
                wtok = load_expert(e + 1, lastuse["w00"], lastuse["w01"],
                                   lastuse["w10"], lastuse["w11"],
                                   lastuse["w2"], lastuse["b"])
            else:
                last_e = lastuse
        self.lr["pb01"] = out_free[0]
        self.lr["pbBwin"] = out_free[1]
        self.lr["pbBlin"] = out_free[1]
        self.lr["pb2"] = h0_free[0]
        self.lr["pb6"] = h0_free[1]
        self.lr["pb3"] = h1_free
        if q + 1 < self.nq:
            self.tk_wq = self.load_mat(self.wq_s, self.wq_h, 8,
                                       after=[last_e["w01"]])
            self.tk_wk = self.load_mat(self.wk_s, self.wk_h, 8,
                                       after=[last_e["w01"]])
            self.tk_wv = self.load_mat(self.wv_s, self.wv_h, 8,
                                       after=[last_e["w11"]])
            wo3h = self.wout_s.rearrange("p (s w) -> p s w", s=16)
            t_wo1 = self.OP("gpsimd", "dma_start", out=wo3h[:, 0:8, :],
                            in_=self.wout_h[0:1024, :].rearrange("(s p) w -> p s w", p=128),
                            after=[last_e["w11"]], inc=("dw", 16))
            t_wo2 = self.OP("gpsimd", "dma_start", out=wo3h[:, 8:16, :],
                            in_=self.wout_h[1024:2048, :].rearrange("(s p) w -> p s w", p=128),
                            after=[last_e["w2"]], inc=("dw", 16))
            self.tk_wo = t_wo2
        self.store_tok = st_toks[-1]
        self.t1_war = self.store_tok
        if q + 1 < self.nq:
            self.pl.op("sync", "nop", after=[self.store_tok])
            self.pl.barrier()
            OP = self.OP
            OP("vector", "memset", self.qphiT2[:, :], 0.0)
            OP("vector", "memset", self.kphiT2[:, :], 0.0)
            OP("vector", "memset",
               self.v_sup.rearrange("p (h c) -> p h c", c=65)[:, :, 64:65], 1.0,
               inc="v")
            self.prev_tts = None
            self.prev_kd = None
            self.prev_yl = None
            self.prev_yw = None
            self.prev_ct = None
            self.prev_wo = None
            self.prev_qpad_read = None
            self.prev_phig = None
            self.sup_done = None

    def build(self):
        self.t1_war = None
        self.prev_w2g = None
        self.prev_tad = None
        self.ktail_tok = None
        self.vtail_tok = None
        self.prelude()
        for q in range(self.nq):
            if "A" in self.parts:
                self.superchunk(2 * q)
                if self.nsup is None or self.nsup > 1:
                    self.superchunk(2 * q + 1)
            if "B" in self.parts:
                self.phase_B(q)
            else:
                self.store_tok = self.pl.op(
                    "sync", "dma_start",
                    out=self.out_h.rearrange("(c p) w -> p c w", p=128)[:, 8 * q:8 * (q + 1), :],
                    in_=self.t1.rearrange("p (c w) -> p c w", c=8),
                    after=[tk for tk in self.ln_tok if tk is not None], inc=("do", 16))
        self.pl.op("sync", "nop", after=[("do", self.pl.cnt["do"])])
        if self.dbg:
            self.pl.op("sync", "nop", after=[("dx", self.pl.cnt["dx"])])

        block = self.ctx.enter_context(self.nc.Block())
        pl = self.pl

        def runner(name):
            def _(eng):
                for f in pl.q[name]:
                    f(eng)
            return _
        block.tensor(runner("tensor"))
        block.vector(runner("vector"))
        block.scalar(runner("scalar"))
        block.gpsimd(runner("gpsimd"))
        block.sync(runner("sync"))
        self.ctx.close()
        return self.nc


def build_program(n_quarters=4, dbg=False, parts="AB", nsup=None):
    return Builder(n_quarters=n_quarters, dbg=dbg, parts=parts, nsup=nsup).build()


def _w2t_ext(inputs):
    w2t = np.asarray(inputs["L0_ffn_w2"]).transpose(0, 2, 1)
    ext = np.zeros((NE, 17 * 128, D), dtype=np.float32)
    ext[:, :HID, :] = w2t
    ext[:, HID, :] = np.asarray(inputs["L0_ffn_b2"])
    return np.ascontiguousarray(ext)


def _prep_inputs(inputs):
    f = np.ascontiguousarray
    x = np.asarray(inputs["L0_attn_x"])
    shared = {
        "wq": f(np.asarray(inputs["L0_attn_Wq"])),
        "wk": f(np.asarray(inputs["L0_attn_Wk"])),
        "wv": f(np.asarray(inputs["L0_attn_Wv"])),
        "wqf": f(np.asarray(inputs["L0_attn_Wqf"])),
        "wkf": f(np.asarray(inputs["L0_attn_Wkf"])),
        "wout": f(np.asarray(inputs["L0_attn_Wout"])),
        "normw": f(np.asarray(inputs["L0_attn_norm_w"]).reshape(1, D)),
        "w0t": f(np.asarray(inputs["L0_ffn_w0"]).transpose(0, 2, 1)),
        "w1t": f(np.asarray(inputs["L0_ffn_w1"]).transpose(0, 2, 1)),
        "w2t": _w2t_ext(inputs),
        "b0r": f(np.asarray(inputs["L0_ffn_b0"]).reshape(NE, 16, 128).transpose(0, 2, 1)),
        "b1r": f(np.asarray(inputs["L0_ffn_b1"]).reshape(NE, 16, 128).transpose(0, 2, 1)),
        "gwt": f(np.asarray(inputs["L0_ffn_gate_w"]).T),
        "gb": f(np.asarray(inputs["L0_ffn_gate_b"]).reshape(NE, 1)),
    }
    in_maps = []
    for c in range(8):
        m = dict(shared)
        m["x"] = f(x[c])
        in_maps.append(m)
    return in_maps


def kernel(**inputs):
    in_maps = _prep_inputs(inputs)
    nc = build_program()
    res = run_bass_kernel_spmd(nc, in_maps, core_ids=list(range(8)))
    out = np.stack([np.asarray(r["out"]) for r in res.results], axis=0)
    return out.astype(np.float32)


# version 20 (final)
# speedup vs baseline: 1.0887x; 1.0220x over previous
import numpy as np
import ml_dtypes
import concourse.bass as bass
import concourse.mybir as mybir
from concourse.bass_utils import run_bass_kernel_spmd
from concourse.alu_op_type import AluOpType as ALU
from contextlib import ExitStack

F32 = mybir.dt.float32
BF16 = mybir.dt.bfloat16
AF = mybir.ActivationFunctionType

T = 4096
D = 1024
NH = 16
FEAT = 16
NPHI = 153
NCH = 32
NE = 8
HID = 2048
EPS_N = 1e-6
EPS_L = 1e-6
EPS_LN = 1e-5
INV_SQRT2 = float(1.0 / np.sqrt(2.0))
WIN_SCALE = 0.125

ENGS = ("tensor", "vector", "scalar", "gpsimd", "sync")


TSP_METHS = {"tensor_scalar", "scalar_tensor_tensor", "reciprocal"}
DMA_SEMS = {"dx", "dw", "do", "dw00", "dw01", "dw10", "dw11", "dw2", "dbb0", "dbb1"}


class Plan:
    def __init__(self):
        self.q = {e: [] for e in ENGS}
        self.cnt = {}
        self.waited = {e: {} for e in ENGS}
        self.sems = {}
        self.tsp_pending = {e: None for e in ENGS}
        self.esem = {"tensor": "t", "vector": "v", "scalar": "s",
                     "gpsimd": "g", "sync": "dx"}
        self.mclass = None

    def new_sem(self, name, handle):
        self.sems[name] = handle
        self.cnt[name] = 0

    def op(self, eng, meth, *args, after=(), inc=None, **kw):
        after = list(after)
        if eng == "tensor" and meth == "matmul":
            m = args[1].shape[-1] if hasattr(args[1], "shape") else 128
            mc = 32 if m <= 32 else (64 if m <= 64 else 128)
            if self.mclass is not None and mc != self.mclass:
                self.q["tensor"].append(lambda e: e.drain())
            self.mclass = mc
        if meth != "nop" and (eng in ("vector", "scalar")
                              or (eng == "gpsimd" and meth != "dma_start")):
            sn = self.esem[eng]
            if self.cnt[sn] > 0:
                after.append((sn, self.cnt[sn]))
            if inc is None:
                inc = sn
            else:
                nm = inc[0] if isinstance(inc, tuple) else inc
                assert nm == sn, (eng, meth, inc)
        waits = []
        for tk in after:
            if tk is None:
                continue
            s, v = tk
            if s in DMA_SEMS:
                v = self.cnt[s]
            if v > self.waited[eng].get(s, 0):
                waits.append((s, v))
                self.waited[eng][s] = v
        tok = None
        incname = None
        incn = 0
        if inc is not None:
            incname, incn = inc if isinstance(inc, tuple) else (inc, 1)
            self.cnt[incname] += incn
            tok = (incname, self.cnt[incname])

        sems = self.sems

        def f(e, waits=waits, meth=meth, args=args, kw=kw, incname=incname, incn=incn):
            for s, v in waits:
                e.wait_ge(sems[s], v)
            r = getattr(e, meth)(*args, **kw)
            if incname is not None:
                r.then_inc(sems[incname], incn)
        self.q[eng].append(f)
        return tok

    def raw(self, eng, fn, after=()):
        waits = []
        for tk in after:
            if tk is None:
                continue
            s, v = tk
            if s in DMA_SEMS:
                v = self.cnt[s]
            if v > self.waited[eng].get(s, 0):
                waits.append((s, v))
                self.waited[eng][s] = v
        sems = self.sems

        def f(e, waits=waits, fn=fn):
            for s, v in waits:
                e.wait_ge(sems[s], v)
            fn(e)
        self.q[eng].append(f)

    def barrier(self):
        toks = {}
        for e in ENGS:
            own = []
            sn = self.esem.get(e)
            if e in ("vector", "scalar", "gpsimd") and self.cnt.get(sn, 0) > 0:
                own = [(sn, self.cnt[sn])]
            toks[e] = self.op(e, "nop", after=own, inc="b_" + e)
        for e in ENGS:
            self.op(e, "nop", after=[toks[o] for o in ENGS if o != e])


class Builder:
    def __init__(self, n_quarters=4, dbg=False, parts="AB", nsup=None):
        self.nq = n_quarters
        self.dbg = dbg
        self.parts = parts
        self.nsup = nsup
        self.stage = 99
        self.natt = 4
        self.castage = 99
        nc = bass.Bass("TRN2", target_bir_lowering=False)
        self.nc = nc
        self.ctx = ExitStack()
        E = self.ctx.enter_context

        self.x_h = nc.dram_tensor("x", [T, D], F32, kind="ExternalInput").ap()
        self.wq_h = nc.dram_tensor("wq", [D, D], F32, kind="ExternalInput").ap()
        self.wk_h = nc.dram_tensor("wk", [D, D], F32, kind="ExternalInput").ap()
        self.wv_h = nc.dram_tensor("wv", [D, D], F32, kind="ExternalInput").ap()
        self.wqf_h = nc.dram_tensor("wqf", [64, FEAT], F32, kind="ExternalInput").ap()
        self.wkf_h = nc.dram_tensor("wkf", [64, FEAT], F32, kind="ExternalInput").ap()
        self.wout_h = nc.dram_tensor("wout", [2 * D, D], F32, kind="ExternalInput").ap()
        self.normw_h = nc.dram_tensor("normw", [1, D], F32, kind="ExternalInput").ap()
        self.w0t_h = nc.dram_tensor("w0t", [NE, D, HID], F32, kind="ExternalInput").ap()
        self.w1t_h = nc.dram_tensor("w1t", [NE, D, HID], F32, kind="ExternalInput").ap()
        self.w2t_h = nc.dram_tensor("w2t", [NE, 17 * 128, D], F32, kind="ExternalInput").ap()
        self.b0r_h = nc.dram_tensor("b0r", [NE, 128, 16], F32, kind="ExternalInput").ap()
        self.b1r_h = nc.dram_tensor("b1r", [NE, 128, 16], F32, kind="ExternalInput").ap()
        self.gwt_h = nc.dram_tensor("gwt", [D, NE], F32, kind="ExternalInput").ap()
        self.gb_h = nc.dram_tensor("gb", [NE, 1], F32, kind="ExternalInput").ap()
        self.out_h = nc.dram_tensor("out", [T, D], F32, kind="ExternalOutput").ap()
        if dbg:
            self.dt1_h = nc.dram_tensor("d_t1", [T, D], F32, kind="ExternalOutput").ap()
            self.dg_h = nc.dram_tensor("d_g", [T, NE], F32, kind="ExternalOutput").ap()

        self.t1 = E(nc.sbuf_tensor("t1", [128, 8 * D], F32))
        self.kv = E(nc.sbuf_tensor("kv", [128, NH * 65], F32))
        self.kv2 = E(nc.sbuf_tensor("kv2", [32, NH * 65], F32))
        self.kvb = E(nc.sbuf_tensor("kvb", [128, NH * 65], BF16))
        self.kvb2 = E(nc.sbuf_tensor("kvb2", [128, NH * 65], BF16))
        self.ktail = E(nc.sbuf_tensor("ktail", [128, 8 * 128], BF16))
        self.vtail = E(nc.sbuf_tensor("vtail", [128, NH * 65], BF16))
        self.normw_b = E(nc.sbuf_tensor("normw_b", [128, D], BF16))
        self.wqf_s = E(nc.sbuf_tensor("wqf_s", [128, 2 * FEAT], BF16))
        self.wkf_s = E(nc.sbuf_tensor("wkf_s", [128, 2 * FEAT], BF16))
        self.r0ones = E(nc.sbuf_tensor("r0ones", [128, 512], BF16))
        self.maskC = E(nc.sbuf_tensor("maskC", [128, 128], BF16))
        self.maskP = E(nc.sbuf_tensor("maskP", [128, 128], BF16))
        self.maskL = E(nc.sbuf_tensor("maskL", [128, 128], BF16))
        self.idbf = E(nc.sbuf_tensor("idbf", [128, 128], BF16))
        self.gwt_s = E(nc.sbuf_tensor("gwt_s", [128, 8 * NE], BF16))
        self.gbb = E(nc.sbuf_tensor("gbb", [128, NE], F32))
        self.smal = E(nc.sbuf_tensor("smal", [128, 32], F32))
        self.top2 = E(nc.sbuf_tensor("top2", [128, 4 * NE], F32))

        AR_KB = 155
        self.arena = E(nc.sbuf_tensor("arena", [128, AR_KB * 512], BF16))
        ar = self.arena.ap()

        def carve(off_kb, nelem, dt, parts=128):
            offb = int(off_kb * 1024)
            if dt == F32:
                assert offb % 4 == 0
                return ar.bitcast(F32)[0:parts, offb // 4: offb // 4 + nelem]
            return ar[0:parts, offb // 2: offb // 2 + nelem]

        self.wq_s = carve(0, 8 * D, BF16)
        self.wk_s = carve(16, 8 * D, BF16)
        self.wv_s = carve(32, 8 * D, BF16)
        self.wout_s = carve(48, 16 * D, BF16)
        self.hT = carve(80, 8 * 512, BF16)
        self.q_dm = carve(88, 8 * 512, BF16)
        self.k_sup = carve(96, 8 * 512, BF16)
        self.v_sup = carve(104, 4 * NH * 65, BF16)
        self.hbf = carve(112.125, D, BF16)
        self.qf_tm = carve(114.125, 256, F32)
        self.qf_h = carve(115.125, 256, F32)
        self.kf_tm = carve(116.125, 256, F32)
        self.kf_h = carve(117.125, 256, F32)
        self.kphi = carve(118.125, NH * NPHI, BF16)
        self.qphi = carve(122.9375, NH * NPHI, BF16)
        self.qphiT = carve(127.75, 4 * 128, BF16)
        self.qphiT2 = carve(128.75, 4 * 128, BF16)
        self.kphiT = carve(129.75, 4 * 128, BF16)
        self.kphiT2 = carve(130.75, 4 * 128, BF16)
        self.sqA = carve(131.75, 4 * 128, BF16)
        self.eC = carve(132.75, 4 * 128, BF16)
        self.eP = carve(133.75, 4 * 128, BF16)
        self.concat = carve(134.75, 2 * D, BF16)
        self.concatT = self.concat
        self.fscratch = carve(138.75, D, F32)
        self.sqscratch = carve(138.75, D, BF16)
        self.q_pad = carve(142.75, NH * 128, BF16)
        self.nwrow = carve(142.75, D, BF16)
        self.xring = [carve(146.75 + 4 * i, D, F32) for i in range(2)]
        self.pw0 = carve(0, 8 * HID, BF16)
        self.pw1 = carve(32, 8 * HID, BF16)
        self.pw2 = carve(64, 17 * D, BF16)
        self.t1T = carve(98, 8 * D, BF16)
        self.act_s = carve(114, 16 * 512, BF16)
        self.h0b_t = [carve(130 + i, 512, BF16) for i in range(2)]
        self.silu_t = [carve(132 + i, 512, BF16) for i in range(2)]
        self.fscrB = carve(134, D, F32)
        self.t1bf = carve(138, D, BF16)
        self.b0col = carve(140, 16, F32)
        self.b1col = carve(140.0625, 16, F32)
        self.scr8 = carve(140.125, NE, F32)
        self.G_s = carve(140.25, 8 * NE, F32)

        self.pbA = E(nc.psum_tensor("pbA", [128, 1024], F32))
        self.pbB = E(nc.psum_tensor("pbB", [128, 1024], F32))
        self.pb2 = E(nc.psum_tensor("pb2", [128, 512], F32))
        self.pb3 = E(nc.psum_tensor("pb3", [128, 512], F32))
        self.pb6 = E(nc.psum_tensor("pb6", [128, 512], F32))
        self.pT = E(nc.psum_tensor("pT", [128, 1024], BF16))
        self.pb01 = self.pbA

        self.pl = Plan()
        for sn in (["t", "v", "s", "g", "dx", "dw", "do",
                    "dw00", "dw01", "dw10", "dw11", "dw2", "dbb0", "dbb1"]
                   + ["b_" + e for e in ENGS]):
            self.pl.new_sem(sn, E(nc.semaphore("sem_" + sn)))

        from concourse.hw_specs import get_activation_tables
        tabs = list(get_activation_tables(nc.m.arch).values())

        def set_id(need):
            for i, fs in enumerate(tabs):
                if all(f in fs for f in need):
                    return i
            raise ValueError(need)
        self.SET_NLE = set_id([AF.Exp, AF.Ln, AF.Square, AF.Identity])
        self.SET_SILU = set_id([AF.Silu, AF.Identity])

        self.lr = {k: None for k in ["pb01", "pbBwin", "pbBlin", "pb2", "pb3", "pb6", "pT"]}
        self.x_tok = [None] * NCH
        self.xfree = [None, None]
        self.kvb_tok = [None, None]
        self.ln_tok = [None] * 8
        self.t2x_tok = [None] * 8
        self.pending_tail = None
        self.pending_ln = None
        self.tk_wq = self.tk_wk = self.tk_wv = self.tk_wo = None
        self.sup_done = None
        self.ktail_tok = None
        self.vtail_tok = None
        self.prev_tts = None
        self.prev_kd = None
        self.prev_yl = None
        self.prev_yw = None
        self.prev_ct = None
        self.prev_wo = None
        self.prev_qpad_read = None
        self.prev_phig = None
        self.store_tok = None
        self.dbg_toks = []

    def prelude(self):
        OP, tsc, col = self.OP, self.tsc, self.col
        mvals = self.fscratch[0:128, 0:128]
        mvals2 = self.fscratch[0:128, 128:256]
        tki1 = OP("gpsimd", "iota", mvals, pattern=[[1, 128]], base=0,
                  channel_multiplier=-1, allow_small_or_imprecise_dtypes=True, inc="g")
        tki2 = OP("gpsimd", "iota", mvals2, pattern=[[-1, 128]], base=-64,
                  channel_multiplier=1, allow_small_or_imprecise_dtypes=True, inc="g")
        tsc("vector", self.maskL[:, :], mvals, 0.0, None, ALU.is_ge, after=[tki1])
        tsc("vector", self.maskC[:, :], mvals, 64.0, None, ALU.is_le)
        OP("vector", "tensor_tensor", self.maskC[:, :], self.maskC[:, :],
           self.maskL[:, :], ALU.mult)
        tsc("vector", self.maskP[:, :], mvals2[:, :], 0.0, None, ALU.is_ge,
            after=[tki2])
        tkid = tsc("vector", self.idbf[:, :], mvals, 0.0, None, ALU.is_equal, inc="v")
        self.tk_idbf = tkid
        tki3 = OP("gpsimd", "iota", mvals, pattern=[[0, 128]], base=0,
                  channel_multiplier=1, allow_small_or_imprecise_dtypes=True,
                  after=[tkid], inc="g")
        t_r0a = tsc("vector", self.r0ones[:, 0:128], mvals, 0.0, None, ALU.is_equal,
                    after=[tki3], inc="v")
        OP("vector", "tensor_copy", self.r0ones[:, 128:256], self.r0ones[:, 0:128])
        self.tk_r0 = OP("vector", "tensor_copy", self.r0ones[:, 256:512],
                        self.r0ones[:, 0:256], inc="v")
        OP("vector", "memset", self.qphiT2[:, :], 0.0)
        OP("vector", "memset", self.kphiT2[:, :], 0.0)
        OP("vector", "memset",
           self.v_sup.rearrange("p (h c) -> p h c", c=65)[:, :, 64:65], 1.0)
        OP("vector", "memset",
           self.vtail.rearrange("p (h c) -> p h c", c=65)[:, :, 64:65], 1.0)
        OP("vector", "memset", self.t1[:, :], 0.0)
        OP("vector", "memset", self.kv[:, :], 0.0)
        OP("vector", "memset", self.kv2[0:25, :], 0.0)
        self.kvb_tok[0] = OP("vector", "memset", self.kvb[:, :], 0.0, inc="v")
        self.kvb_tok[1] = OP("vector", "memset", self.kvb2[:, :], 0.0, inc="v")

        self.tk_wq = self.load_mat(self.wq_s, self.wq_h, 8)
        self.tk_wk = self.load_mat(self.wk_s, self.wk_h, 8)
        self.tk_wv = self.load_mat(self.wv_s, self.wv_h, 8)
        self.tk_wo = self.load_mat(self.wout_s, self.wout_h, 16)
        t_z1 = OP("vector", "memset", self.wqf_s[:, :], 0.0, inc="v")
        t_z2 = OP("vector", "memset", self.wkf_s[:, :], 0.0, inc="v")
        t1_ = OP("gpsimd", "dma_start", out=self.wqf_s[0:64, 0:FEAT],
                 in_=self.wqf_h[:, :], after=[t_z1], inc=("dw", 16))
        t2_ = OP("gpsimd", "dma_start", out=self.wqf_s[64:128, FEAT:2 * FEAT],
                 in_=self.wqf_h[:, :], inc=("dw", 16))
        self.tk_wqf = t2_
        t3_ = OP("gpsimd", "dma_start", out=self.wkf_s[0:64, 0:FEAT],
                 in_=self.wkf_h[:, :], after=[t_z2], inc=("dw", 16))
        t4_ = OP("gpsimd", "dma_start", out=self.wkf_s[64:128, FEAT:2 * FEAT],
                 in_=self.wkf_h[:, :], inc=("dw", 16))
        self.tk_wkf = t4_
        self.tk_gw = OP("gpsimd", "dma_start",
                        out=self.gwt_s.rearrange("p (s w) -> p s w", s=8),
                        in_=self.gwt_h.rearrange("(s p) w -> p s w", p=128),
                        inc=("dw", 16))
        t_zn = OP("vector", "memset", self.nwrow[:, :], 0.0, inc="v")
        tk_nw = OP("gpsimd", "dma_start", out=self.nwrow[0:1, :],
                   in_=self.normw_h[:, :], after=[t_zn], inc=("dw", 16))
        t_zg = OP("vector", "memset", self.t1bf[:, 0:NE], 0.0, inc="v")
        tk_gbl = OP("gpsimd", "dma_start", out=self.t1bf[0:1, 0:NE],
                    in_=self.gb_h.rearrange("e one -> one e"), after=[t_zg],
                    inc=("dw", 16))
        OP("tensor", "matmul", self.pb01[:, 0:512], self.r0ones[:, 0:128],
           self.nwrow[:, 0:512], start=True, stop=True,
           after=[tk_nw, self.tk_r0, self.kvb_tok[1]])
        tnb = OP("tensor", "matmul", self.pb01[:, 512:1024], self.r0ones[:, 0:128],
                 self.nwrow[:, 512:1024], start=True, stop=True, inc="t")
        self.tk_nbe = OP("vector", "tensor_copy", self.normw_b[:, :], self.pb01[:, :],
                         after=[tnb], inc="v")
        OP("vector", "memset", self.q_pad[:, :], 0.0, after=[tnb])
        tgb = OP("tensor", "matmul", self.pb2[:, 0:NE], self.r0ones[:, 0:128],
                 self.t1bf[:, 0:NE], start=True, stop=True, after=[tk_gbl], inc="t")
        self.tk_gb = OP("vector", "tensor_copy", self.gbb[:, :], self.pb2[:, 0:NE],
                        after=[tgb], inc="v")
        self.lr["pb01"] = self.tk_nbe
        self.lr["pb2"] = self.tk_gb
        self.load_act(self.SET_NLE)
        self.load_x(0)
        self.load_x(1)
        self.pl.barrier()

    def superchunk(self, S):
        OP = self.OP
        hT3 = self.hT.rearrange("p (s w) -> p s w", s=8)
        h_toks = []
        for c in range(4):
            C = 4 * S + c
            xa = self.xring[C % 2]
            t_sq = OP("scalar", "activation", self.sqscratch[:, :], xa[:, :], AF.Square,
                      accum_out=self.col(6), after=[self.x_tok[C], self.sup_done],
                      inc="s")
            t_ms = self.tsc("vector", self.col(7), self.col(6), 1.0 / D, EPS_N,
                            ALU.mult, ALU.add, after=[t_sq], inc="v")
            t_ln = OP("scalar", "activation", self.col(4), self.col(7), AF.Ln,
                      after=[t_ms], inc="s")
            t_rs = OP("scalar", "activation", self.col(5), self.col(4), AF.Exp,
                      scale=-0.5, after=[t_ln], inc="s")
            t_h1 = OP("scalar", "activation", self.hbf[:, :], xa[:, :], AF.Identity,
                      scale=self.col(5), after=[t_rs, self.prev_tts], inc="s")
            cq = C % 8
            t_2x = OP("scalar", "activation", self.t1[:, D * cq:D * (cq + 1)],
                      xa[:, :], AF.Identity, scale=2.0,
                      after=[self.store_tok, self.t1_war], inc="s")
            self.t2x_tok[cq] = t_2x
            self.xfree[C % 2] = t_2x
            self.load_x(C + 2)
            t_h2 = OP("vector", "tensor_tensor", self.hbf[:, :], self.hbf[:, :],
                      self.normw_b[:, :], ALU.mult, after=[t_h1, self.tk_nbe], inc="v")
            tts = None
            for j in range(8):
                tts = OP("tensor", "transpose", self.pT[:, 128 * j:128 * (j + 1)],
                         self.hbf[:, 128 * j:128 * (j + 1)], self.idbf[:, :],
                         after=[t_h2, self.tk_idbf, self.lr["pT"]],
                         inc=("t" if j == 7 else None))
            self.prev_tts = tts
            t_ev = OP("vector", "tensor_copy", hT3[:, :, 128 * c:128 * (c + 1)],
                      self.pT.rearrange("p (s w) -> p s w", s=8),
                      after=[tts, self.sup_done], inc="v")
            self.lr["pT"] = t_ev
            h_toks.append(t_ev)
        h_all = h_toks[-1]
        if self.stage < 2:
            return
        pp = [self.lr["pb01"], self.lr["pb01"]]
        wq3 = self.wq_s.rearrange("p (s w) -> p s w", s=8)
        wk3 = self.wk_s.rearrange("p (s w) -> p s w", s=8)
        wv3 = self.wv_s.rearrange("p (s w) -> p s w", s=8)
        k3 = self.k_sup.rearrange("p (s w) -> p s w", s=8)
        for mj in range(8):
            hf = mj % 2
            tk = None
            for kk in range(8):
                tk = OP("tensor", "matmul", self.pb01[:, 512 * hf:512 * (hf + 1)],
                        wq3[:, kk, 128 * mj:128 * (mj + 1)], hT3[:, kk, :],
                        start=(kk == 0), stop=(kk == 7),
                        after=[self.tk_wq, h_all, pp[hf]],
                        inc=("t" if kk == 7 else None))
            pp[hf] = OP("vector", "tensor_copy", self.q_dm[:, 512 * mj:512 * (mj + 1)],
                        self.pb01[:, 512 * hf:512 * (hf + 1)],
                        after=[tk, self.sup_done], inc="v")
        tk_q = (pp[0], pp[1])
        for mj in range(8):
            hf = mj % 2
            tk = None
            for kk in range(8):
                tk = OP("tensor", "matmul", self.pb01[:, 512 * hf:512 * (hf + 1)],
                        wk3[:, kk, 128 * mj:128 * (mj + 1)], hT3[:, kk, :],
                        start=(kk == 0), stop=(kk == 7),
                        after=[self.tk_wk, pp[hf]],
                        inc=("t" if kk == 7 else None))
            pp[hf] = OP("vector", "tensor_copy", self.k_sup[:, 512 * mj:512 * (mj + 1)],
                        self.pb01[:, 512 * hf:512 * (hf + 1)],
                        after=[tk, self.sup_done], inc="v")
        tk_k = (pp[0], pp[1])
        v65 = self.v_sup.rearrange("p (c h e) -> p c h e", c=4, e=65)
        for c in range(4):
            for nh in range(2):
                hf = nh
                tk = None
                for kk in range(8):
                    tk = OP("tensor", "matmul", self.pb01[:, 512 * hf:512 * (hf + 1)],
                            hT3[:, kk, 128 * c:128 * (c + 1)],
                            wv3[:, kk, 512 * nh:512 * (nh + 1)],
                            start=(kk == 0), stop=(kk == 7),
                            after=[self.tk_wv, pp[hf]],
                            inc=("t" if kk == 7 else None))
                pp[hf] = OP("vector", "tensor_copy",
                            v65[:, c, 8 * nh:8 * (nh + 1), 0:64],
                            self.pb01[:, 512 * hf:512 * (hf + 1)]
                                .rearrange("p (h e) -> p h e", e=64),
                            after=[tk, self.sup_done], inc="v")
        tk_v = (pp[0], pp[1])
        self.lr["pb01"] = pp[1]
        if self.stage < 3:
            return
        for c in range(min(4, self.natt)):
            self.chunk_attn(S, c, 4 * S + c, tk_q, tk_k, tk_v)
        kt = OP("vector", "tensor_copy",
                self.ktail.rearrange("p (s w) -> p s w", s=8),
                k3[:, :, 384:512], inc="v")
        vt = OP("vector", "tensor_copy", self.vtail[:, :],
                self.v_sup[:, 3 * NH * 65:4 * NH * 65], inc="v")
        self.ktail_tok = kt
        self.vtail_tok = vt
        self.sup_done = self.prev_kd

    def chunk_attn(self, S, c, C, tk_q, tk_k, tk_v):
        OP, tsc = self.OP, self.tsc
        vr = self.v_sup[:, c * NH * 65:(c + 1) * NH * 65]
        first_c = (c == 0)
        vprev = self.vtail if first_c else \
            self.v_sup[:, (c - 1) * NH * 65:c * NH * 65]

        def kslice(h_or_sl, width=128, prev=False, whole_slice=True, off=0):
            sl = h_or_sl
            if prev and first_c:
                return self.ktail[:, 128 * sl + off:128 * sl + off + width]
            base = 512 * sl + 128 * (c - 1 if prev else c)
            return self.k_sup[:, base + off:base + off + width]

        qp3 = self.q_pad.rearrange("p (s two w) -> p s two w", s=8, two=2)
        qd3 = self.q_dm.rearrange("p (s w) -> p s w", s=8)
        OP("gpsimd", "tensor_copy", qp3[0:64, :, 0, :],
           qd3[0:64, :, 128 * c:128 * (c + 1)],
           after=[tk_q[0], tk_q[1], self.prev_qpad_read])
        tqp = OP("gpsimd", "tensor_copy", qp3[64:128, :, 1, :],
                 qd3[64:128, :, 128 * c:128 * (c + 1)])
        tk1 = None
        for sl in range(8):
            tk1 = OP("tensor", "matmul", self.pb2[:, 32 * sl:32 * (sl + 1)],
                     self.q_dm[:, 512 * sl + 128 * c:512 * sl + 128 * (c + 1)],
                     self.wqf_s[:, :], start=True, stop=True,
                     after=[tk_q[0], tk_q[1], self.tk_wqf, self.lr["pb2"]],
                     inc=("t" if sl == 7 else None))
        t_qftm = OP("vector", "tensor_copy", self.qf_tm[:, :], self.pb2[:, 0:256],
                    after=[tk1], inc="v")
        tk2 = None
        for sl in range(8):
            tk2 = OP("tensor", "matmul", self.pb2[:, 256 + 32 * sl:256 + 32 * (sl + 1)],
                     kslice(sl), self.wkf_s[:, :], start=True, stop=True,
                     after=[tk_k[0], tk_k[1], self.tk_wkf, t_qftm],
                     inc=("t" if sl == 7 else None))
        t_kftm = OP("vector", "tensor_copy", self.kf_tm[:, :], self.pb2[:, 256:512],
                    after=[tk2], inc="v")
        self.lr["pb2"] = t_kftm
        self.emit_tail()
        if self.castage < 1:
            return
        if self.castage < 2:
            return
        tsc("vector", self.qf_h[:, :], self.qf_tm[:, :], INV_SQRT2, None, ALU.mult,
            after=[self.prev_phig])
        t_sc = tsc("vector", self.kf_h[:, :], self.kf_tm[:, :], INV_SQRT2, None,
                   ALU.mult, inc="v")

        k3d = self.kphi.rearrange("p (h w) -> p h w", h=NH)
        q3d = self.qphi.rearrange("p (h w) -> p h w", h=NH)
        ktm3 = self.kf_tm.rearrange("p (h w) -> p h w", h=NH)
        qtm3 = self.qf_tm.rearrange("p (h w) -> p h w", h=NH)
        khh3 = self.kf_h.rearrange("p (h w) -> p h w", h=NH)
        qhh3 = self.qf_h.rearrange("p (h w) -> p h w", h=NH)
        OP("vector", "memset", k3d[:, :, 0:1], 1.0, after=[self.prev_kd])
        OP("vector", "memset", q3d[:, :, 0:1], 1.0)
        OP("vector", "tensor_copy", k3d[:, :, 1:17], ktm3[:, :, :])
        OP("vector", "tensor_copy", q3d[:, :, 1:17], qtm3[:, :, :])
        def _two_sided(ap_k, ap_q):
            d = ap_q.offset - ap_k.offset
            return bass.AP(tensor=ap_k.tensor, offset=ap_k.offset,
                           ap=[list(p) for p in ap_k.ap[:1]] +
                              [[d, 2]] + [list(p) for p in ap_k.ap[1:]])
        off = 17
        for i in range(FEAT):
            n = FEAT - i
            if n > 1:
                outv = _two_sided(k3d[:, :, off + 1:off + n],
                                  q3d[:, :, off + 1:off + n])
                in0v = _two_sided(
                    khh3[:, :, i:i + 1].broadcast_to([128, NH, n - 1]),
                    qhh3[:, :, i:i + 1].broadcast_to([128, NH, n - 1]))
                in1v = _two_sided(ktm3[:, :, i + 1:FEAT], qtm3[:, :, i + 1:FEAT])
                OP("vector", "tensor_tensor", outv, in0v, in1v, ALU.mult)
            off += n
        off = 17
        t_phiv = None
        for i in range(FEAT):
            n = FEAT - i
            outd = _two_sided(k3d[:, :, off:off + 1], q3d[:, :, off:off + 1])
            ind = _two_sided(khh3[:, :, i:i + 1], qhh3[:, :, i:i + 1])
            t_phiv = OP("vector", "tensor_tensor", outd, ind, ind, ALU.mult,
                        inc=("v" if i == FEAT - 1 else None))
            off += n
        t_phig = t_phiv
        self.prev_phig = t_phig
        t_phi = t_phiv
        def ywin_bank(hg):
            return (self.pb2, 0, "pb2") if hg % 2 == 0 else (self.pbB, 512, "pbBwin")

        def emit_win_norm(hg, tyw):
            pb, base, key = ywin_bank(hg)
            zwin = pb[:, base:base + 260].rearrange(
                "p (j w) -> p j w", w=65)[:, :, 64:65]
            tlz = OP("scalar", "activation", self.smal[:, 20:24],
                     zwin.squeeze(2), AF.Ln, after=[tyw], inc="s")
            rcw = OP("scalar", "activation", self.smal[:, 16:20],
                     self.smal[:, 20:24], AF.Exp, scale=-1.0, inc="s")
            c3 = self.concat.rearrange("p (half h w) -> p half h w", half=2, w=64)
            tnw = None
            for j in range(4):
                tnw = OP("scalar", "activation", c3[:, 1, 4 * hg + j, :],
                         pb[:, base + 65 * j:base + 65 * j + 64],
                         AF.Identity, scale=self.smal[:, 16 + j:17 + j],
                         after=[rcw, self.prev_ct], inc="s")
            self.lr[key] = tnw
            return tnw
        win_toks = []
        pend_norm = None
        for hg in range(4):
            hs4 = [4 * hg + j for j in range(4)]
            tsC = None
            for j, h in enumerate(hs4):
                tsC = OP("tensor", "matmul", self.pb3[:, 128 * j:128 * (j + 1)],
                         kslice(h // 2), self.q_pad[:, 128 * h:128 * (h + 1)],
                         start=True, stop=True,
                         after=[tqp, tk_k[0], tk_k[1], self.lr["pb3"]],
                         inc=("t" if j == 3 else None))
            teC = OP("scalar", "activation", self.eC[:, :], self.pb3[:, :],
                     AF.Exp, scale=WIN_SCALE, after=[tsC, self.prev_yw], inc="s")
            teCm = OP("gpsimd", "tensor_tensor",
                      self.eC.rearrange("p (j w) -> p j w", j=4),
                      self.eC.rearrange("p (j w) -> p j w", j=4),
                      self.maskC[:, :].unsqueeze(1).broadcast_to([128, 4, 128]),
                      ALU.mult, after=[teC])
            self.lr["pb3"] = teCm
            tePm = None
            if C > 0:
                tsP = None
                for j, h in enumerate(hs4):
                    tsP = OP("tensor", "matmul",
                             self.pb6[:, 128 * j:128 * (j + 1)],
                             kslice(h // 2, prev=True),
                             self.q_pad[:, 128 * h:128 * (h + 1)],
                             start=True, stop=True,
                             after=[self.lr["pb6"],
                                    self.ktail_tok if first_c else None],
                             inc=("t" if j == 3 else None))
                teP = OP("scalar", "activation", self.eP[:, :], self.pb6[:, :],
                         AF.Exp, scale=WIN_SCALE, after=[tsP, self.prev_yw], inc="s")
                tePm = OP("gpsimd", "tensor_tensor",
                          self.eP.rearrange("p (j w) -> p j w", j=4),
                          self.eP.rearrange("p (j w) -> p j w", j=4),
                          self.maskP[:, :].unsqueeze(1).broadcast_to([128, 4, 128]),
                          ALU.mult, after=[teP])
                self.lr["pb6"] = tePm
            self.prev_qpad_read = tsP if C > 0 else tsC
            tyw = None
            ywpb, ywbase, ywkey = ywin_bank(hg)
            for j, h in enumerate(hs4):
                tk = OP("tensor", "matmul",
                        ywpb[:, ywbase + 65 * j:ywbase + 65 * (j + 1)],
                        self.eC[:, 128 * j:128 * (j + 1)], vr[:, 65 * h:65 * (h + 1)],
                        start=True, stop=(C == 0),
                        after=[teCm, tk_v[0], tk_v[1], self.lr[ywkey]],
                        inc=("t" if (C == 0 and j == 3) else None))
                if C > 0:
                    tk = OP("tensor", "matmul",
                            ywpb[:, ywbase + 65 * j:ywbase + 65 * (j + 1)],
                            self.eP[:, 128 * j:128 * (j + 1)],
                            vprev[:, 65 * h:65 * (h + 1)],
                            start=False, stop=True,
                            after=[tePm, self.vtail_tok if first_c else None],
                            inc=("t" if j == 3 else None))
                tyw = tk
            self.prev_yw = tyw
            if pend_norm is not None:
                win_toks.append(emit_win_norm(*pend_norm))
            pend_norm = (hg, tyw)
        win_toks.append(emit_win_norm(*pend_norm))
        if self.castage < 5:
            return
        concat_last = None
        t_kd2 = None
        for hg in range(4):
            hs4 = [4 * hg + j for j in range(4)]
            for phi_src, dT, dT2 in ((self.qphi, self.qphiT, self.qphiT2),
                                     (self.kphi, self.kphiT, self.kphiT2)):
                ttr = None
                for j, h in enumerate(hs4):
                    ttr = OP("tensor", "transpose", self.pT[:, 128 * j:128 * (j + 1)],
                             phi_src[:, NPHI * h:NPHI * h + 128], self.idbf[:, :],
                             after=[t_phi, t_phig, self.lr["pT"], self.prev_yl])
                for j, h in enumerate(hs4):
                    ttr = OP("tensor", "transpose",
                             self.pT[0:25, 512 + 128 * j:512 + 128 * (j + 1)],
                             phi_src[:, NPHI * h + 128:NPHI * h + 153], self.idbf[:, :],
                             inc=("t" if j == 3 else None))
                OP("scalar", "activation", dT[:, :], self.pT[:, 0:512], AF.Identity,
                   after=[ttr, self.prev_yl])
                t_pe = OP("scalar", "activation", dT2[0:25, :],
                          self.pT[0:25, 512:1024], AF.Identity, inc="s")
                self.lr["pT"] = t_pe
            tqk = None
            for j in range(4):
                OP("tensor", "matmul", self.pb3[:, 128 * j:128 * (j + 1)],
                   self.kphiT[:, 128 * j:128 * (j + 1)],
                   self.qphiT[:, 128 * j:128 * (j + 1)],
                   start=True, stop=False, after=[t_pe, self.lr["pb3"]])
                tqk = OP("tensor", "matmul", self.pb3[:, 128 * j:128 * (j + 1)],
                         self.kphiT2[:, 128 * j:128 * (j + 1)],
                         self.qphiT2[:, 128 * j:128 * (j + 1)],
                         start=False, stop=True,
                         inc=("t" if j == 3 else None))
            tAe = OP("scalar", "activation", self.sqA[:, :], self.pb3[:, :],
                     AF.Identity, after=[tqk, self.prev_yl], inc="s")
            tAm = OP("gpsimd", "tensor_tensor",
                     self.sqA.rearrange("p (j w) -> p j w", j=4),
                     self.sqA.rearrange("p (j w) -> p j w", j=4),
                     self.maskL[:, :].unsqueeze(1).broadcast_to([128, 4, 128]),
                     ALU.mult, after=[tAe])
            self.lr["pb3"] = tAe
            tyl = None
            for j, h in enumerate(hs4):
                OP("tensor", "matmul", self.pbB[:, 65 * j:65 * (j + 1)],
                   self.sqA[:, 128 * j:128 * (j + 1)], vr[:, 65 * h:65 * (h + 1)],
                   start=True, stop=False,
                   after=[tAm, self.lr["pbBlin"]])
                OP("tensor", "matmul", self.pbB[:, 65 * j:65 * (j + 1)],
                   self.qphiT[:, 128 * j:128 * (j + 1)],
                   self.kvb[:, 65 * h:65 * (h + 1)],
                   start=False, stop=False, after=[self.kvb_tok[0]])
                tyl = OP("tensor", "matmul", self.pbB[:, 65 * j:65 * (j + 1)],
                         self.qphiT2[:, 128 * j:128 * (j + 1)],
                         self.kvb2[:, 65 * h:65 * (h + 1)],
                         start=False, stop=True, after=[self.kvb_tok[1]],
                         inc=("t" if j == 3 else None))
            self.prev_yl = tyl
            if self.castage < 8:
                continue
            t_kd = None
            for j, h in enumerate(hs4):
                t_kd = OP("tensor", "matmul", self.pb6[:, 65 * j:65 * (j + 1)],
                          self.kphi[:, NPHI * h:NPHI * h + 128],
                          vr[:, 65 * h:65 * (h + 1)],
                          start=True, stop=True,
                          after=[t_phi, t_phig, tyl, self.lr["pb6"]],
                          inc=("t" if j == 3 else None))
            t_kd2 = None
            for j, h in enumerate(hs4):
                t_kd2 = OP("tensor", "matmul", self.pb3[0:25, 65 * j:65 * (j + 1)],
                           self.kphi[:, NPHI * h + 128:NPHI * h + 153],
                           vr[:, 65 * h:65 * (h + 1)],
                           start=True, stop=True, after=[self.lr["pb3"]],
                           inc=("t" if j == 3 else None))
            if self.castage < 9:
                continue
            zlin = self.pbB[:, 0:260].rearrange("p (j w) -> p j w", w=65)[:, :, 64:65]
            tsc("vector", self.smal[:, 8:12], zlin.squeeze(2), 1.0, EPS_L,
                ALU.mult, ALU.add, after=[tyl])
            rc = OP("vector", "reciprocal", self.smal[:, 12:16], self.smal[:, 8:12],
                    inc="v")
            c3 = self.concat.rearrange("p (half h w) -> p half h w", half=2, w=64)
            tnl = OP("vector", "tensor_tensor", c3[:, 0, 4 * hg:4 * hg + 4, :],
                     self.pbB[:, 0:260].rearrange("p (j w) -> p j w", w=65)[:, :, 0:64],
                     self.smal[:, 12:16].unsqueeze(2).broadcast_to([128, 4, 64]),
                     ALU.mult, after=[rc, self.prev_ct], inc="v")
            self.lr["pbBlin"] = tnl
            concat_last = tnl
            kv3 = self.kv.rearrange("p (h e) -> p h e", e=65)
            kv23 = self.kv2.rearrange("p (h e) -> p h e", e=65)
            OP("vector", "tensor_tensor", kv3[:, 4 * hg:4 * hg + 4, :],
               kv3[:, 4 * hg:4 * hg + 4, :],
               self.pb6[:, 0:260].rearrange("p (j w) -> p j w", w=65),
               ALU.add, after=[t_kd])
            tkva = OP("vector", "tensor_tensor", kv23[0:25, 4 * hg:4 * hg + 4, :],
                      kv23[0:25, 4 * hg:4 * hg + 4, :],
                      self.pb3[0:25, 0:260].rearrange("p (j w) -> p j w", w=65),
                      ALU.add, after=[t_kd2], inc="v")
            self.lr["pb6"] = tkva
            self.lr["pb3"] = tkva
        if self.castage < 10:
            return
        self.prev_kd = t_kd2
        self.emit_ln()
        self.pending_tail = (C, concat_last, win_toks[-1], self.prev_yl)

    def OP(self, *a, **k):
        return self.pl.op(*a, **k)

    def tsc(self, eng, out, in0, s1, s2, op0, op1=None, after=(), inc=None, accum=None):
        if accum is not None:
            return self.OP(eng, "tensor_scalar", out, in0, s1, None, op0, op1,
                           accum_out=accum, after=after, inc=inc)
        if op1 is None:
            return self.OP(eng, "tensor_scalar", out, in0, s1, None, op0,
                           after=after, inc=inc)
        return self.OP(eng, "tensor_scalar", out, in0, s1, s2, op0, op1,
                       after=after, inc=inc)

    def col(self, i):
        return self.smal[:, i:i + 1]

    def load_act(self, set_id, after=()):
        nc = self.nc

        def fn(e):
            e.add_instruction(mybir.InstLoadActFuncSet(
                name=nc.get_next_instruction_name(),
                act_func_set_id=set_id, ins=[], outs=[]))
        self.pl.raw("scalar", fn, after=after)

    def load_mat(self, dst, src, nslice, after=()):
        return self.OP("gpsimd", "dma_start",
                       out=dst.rearrange("p (s w) -> p s w", s=nslice),
                       in_=src.rearrange("(s p) w -> p s w", p=128),
                       after=after, inc=("dw", 16))

    def load_x(self, C, after_extra=()):
        if C >= self.nq * 8:
            return
        self.x_tok[C] = self.OP(
            "sync", "dma_start", out=self.xring[C % 2][:, :],
            in_=self.x_h[C * 128:(C + 1) * 128, :],
            after=[self.xfree[C % 2]] + list(after_extra), inc=("dx", 16))

    def emit_tail(self):
        if self.pending_tail is None:
            return
        C, concat_last, tnw_tok, yl_tok = self.pending_tail
        self.pending_tail = None
        self.pending_ln = None
        OP, tsc = self.OP, self.tsc
        self.kvb_tok[0] = OP("vector", "tensor_copy", self.kvb[:, :], self.kv[:, :],
                             after=[yl_tok], inc="v")
        self.kvb_tok[1] = OP("vector", "tensor_copy", self.kvb2[0:25, :],
                             self.kv2[0:25, :], inc="v")
        tcte = None
        for half in range(2):
            tct = None
            for j in range(8):
                tct = OP("tensor", "transpose", self.pT[:, 128 * j:128 * (j + 1)],
                         self.concat[:, 1024 * half + 128 * j:1024 * half + 128 * (j + 1)],
                         self.idbf[:, :],
                         after=[concat_last, tnw_tok, self.lr["pT"]],
                         inc=("t" if j == 7 else None))
            tcte = OP("scalar", "activation",
                      self.concatT[:, 1024 * half:1024 * (half + 1)], self.pT[:, :],
                      AF.Identity, after=[tct, self.prev_wo], inc="s")
            self.lr["pT"] = tcte
        wo3 = self.wout_s.rearrange("p (s w) -> p s w", s=16)
        two = None
        for half in range(2):
            for kk in range(16):
                two = OP("tensor", "matmul", self.pb01[:, 512 * half:512 * (half + 1)],
                         self.concatT[:, 128 * kk:128 * (kk + 1)],
                         wo3[:, kk, 512 * half:512 * (half + 1)],
                         start=(kk == 0), stop=(kk == 15),
                         after=[tcte, self.tk_wo, self.lr["pb01"]],
                         inc=("t" if (kk == 15 and half == 1) else None))
        self.prev_wo = two
        self.prev_ct = two
        cq = C % 8
        t1sl = self.t1[:, D * cq:D * (cq + 1)]
        tres = OP("vector", "tensor_tensor", t1sl, t1sl, self.pb01[:, :], ALU.add,
                  after=[two, self.t2x_tok[cq]], inc="v")
        self.lr["pb01"] = tres
        self.pending_ln = (cq, tres)

    def emit_ln(self):
        if self.pending_ln is None:
            return
        cq, tres = self.pending_ln
        self.pending_ln = None
        OP = self.OP
        t1sl = self.t1[:, D * cq:D * (cq + 1)]
        tm = OP("scalar", "activation", self.sqscratch[:, :], t1sl, AF.Identity,
                accum_out=self.col(0), after=[tres], inc="s")
        tnm = self.tsc("vector", self.col(1), self.col(0), -1.0 / D, None, ALU.mult,
                       after=[tm], inc="v")
        tv = OP("scalar", "activation", self.sqscratch[:, :], t1sl, AF.Square,
                bias=self.col(1), accum_out=self.col(2), after=[tnm], inc="s")
        tve = self.tsc("vector", self.col(3), self.col(2), 1.0 / D, EPS_LN,
                       ALU.mult, ALU.add, after=[tv], inc="v")
        tl = OP("scalar", "activation", self.col(4), self.col(3), AF.Ln,
                after=[tve], inc="s")
        tr = OP("scalar", "activation", self.col(5), self.col(4), AF.Exp, scale=-0.5,
                after=[tl], inc="s")
        tc = OP("scalar", "activation", self.fscratch[:, :], t1sl, AF.Identity,
                bias=self.col(1), after=[tr], inc="s")
        tf = OP("scalar", "activation", t1sl, self.fscratch[:, :], AF.Identity,
                scale=self.col(5), after=[tc], inc="s")
        self.ln_tok[cq] = tf


    def final_ln_chunk(self, q, j, acc):
        OP = self.OP
        t1sl = self.t1[:, D * j:D * (j + 1)]
        tm = OP("scalar", "activation", self.t1bf[:, :], t1sl, AF.Identity,
                accum_out=self.col(0), after=[acc], inc="s")
        tnm = self.tsc("vector", self.col(1), self.col(0), -1.0 / D, None,
                        ALU.mult, after=[tm], inc="v")
        tv = OP("scalar", "activation", self.t1bf[:, :], t1sl, AF.Square,
                bias=self.col(1), accum_out=self.col(2), after=[tnm], inc="s")
        tve = self.tsc("vector", self.col(3), self.col(2), 1.0 / D, EPS_LN,
                        ALU.mult, ALU.add, after=[tv], inc="v")
        tl = OP("scalar", "activation", self.col(4), self.col(3), AF.Ln,
                after=[tve], inc="s")
        tr = OP("scalar", "activation", self.col(5), self.col(4), AF.Exp,
                scale=-0.5, after=[tl], inc="s")
        tcn = OP("scalar", "activation", self.fscrB[:, :], t1sl, AF.Identity,
                 bias=self.col(1), after=[tr], inc="s")
        tf = OP("scalar", "activation", t1sl, self.fscrB[:, :], AF.Identity,
                scale=self.col(5), after=[tcn], inc="s")
        self.ln_tok[j] = tf
        st = OP("sync", "dma_start",
                out=self.out_h.rearrange("(c p) w -> p c w", p=128)[:, 8 * q + j, :],
                in_=self.t1.rearrange("p (c w) -> p c w", c=8)[:, j, :],
                after=[tf], inc=("do", 16))
        return st

    def phase_B(self, q):
        OP, tsc = self.OP, self.tsc
        pl = self.pl
        self.emit_tail()
        self.emit_ln()
        pl.barrier()
        t1T3 = self.t1T.rearrange("p (s w) -> p s w", s=8)
        act3 = self.act_s.rearrange("p (s w) -> p s w", s=16)
        pw03 = self.pw0.rearrange("p (s w) -> p s w", s=8)
        pw13 = self.pw1.rearrange("p (s w) -> p s w", s=8)
        pw23 = self.pw2.rearrange("p (s w) -> p s w", s=17)

        def load_expert(e, a00, a01, a10, a11, a2, ab):
            t00 = OP("gpsimd", "dma_start", out=pw03[:, :, 0:1024],
                     in_=self.w0t_h[e, :, 0:1024].rearrange("(s p) w -> p s w", p=128),
                     after=[a00], inc=("dw00", 16))
            t10 = OP("gpsimd", "dma_start", out=pw13[:, :, 0:1024],
                     in_=self.w1t_h[e, :, 0:1024].rearrange("(s p) w -> p s w", p=128),
                     after=[a10], inc=("dw10", 16))
            t01 = OP("gpsimd", "dma_start", out=pw03[:, :, 1024:2048],
                     in_=self.w0t_h[e, :, 1024:2048].rearrange("(s p) w -> p s w", p=128),
                     after=[a01], inc=("dw01", 16))
            t11 = OP("gpsimd", "dma_start", out=pw13[:, :, 1024:2048],
                     in_=self.w1t_h[e, :, 1024:2048].rearrange("(s p) w -> p s w", p=128),
                     after=[a11], inc=("dw11", 16))
            t2 = OP("gpsimd", "dma_start", out=pw23,
                    in_=self.w2t_h[e, :, :].rearrange("(s p) w -> p s w", p=128),
                    after=[a2], inc=("dw2", 16))
            tb0 = OP("sync", "dma_start", out=self.b0col[:, :],
                     in_=self.b0r_h[e, :, :], after=[ab], inc=("dbb0", 16))
            tb1 = OP("sync", "dma_start", out=self.b1col[:, :],
                     in_=self.b1r_h[e, :, :], after=[ab], inc=("dbb1", 16))
            return {"w00": t00, "w01": t01, "w10": t10, "w11": t11,
                    "w2": t2, "b0": tb0, "b1": tb1}

        wtok = load_expert(0, None, None, None, None, None, None)

        trt = None
        trts = []
        for j in range(8):
            tcast = tsc("vector", self.t1bf[:, :], self.t1[:, D * j:D * (j + 1)],
                        1.0, None, ALU.mult, after=[self.ln_tok[j]], inc="v")
            ttp = None
            for kk in range(8):
                ttp = OP("tensor", "transpose", self.pT[:, 128 * kk:128 * (kk + 1)],
                         self.t1bf[:, 128 * kk:128 * (kk + 1)], self.idbf[:, :],
                         after=[tcast, self.lr["pT"]],
                         inc=("t" if kk == 7 else None))
            trt = OP("vector", "tensor_copy", t1T3[:, :, 128 * j:128 * (j + 1)],
                     self.pT.rearrange("p (s w) -> p s w", s=8),
                     after=[ttp], inc="v")
            self.lr["pT"] = trt
            trts.append(trt)
        gw3 = self.gwt_s.rearrange("p (s w) -> p s w", s=8)
        lg = self.top2[:, 0:8]
        is1 = self.top2[:, 8:16]
        maskd = self.top2[:, 16:24]
        is2 = self.top2[:, 24:32]
        g_tok = None
        for j in range(8):
            gpb, gkey = (self.pb2, "pb2") if j % 2 == 0 else (self.pb3, "pb3")
            tlg = None
            for kk in range(8):
                tlg = OP("tensor", "matmul", gpb[:, 0:NE],
                         t1T3[:, kk, 128 * j:128 * (j + 1)], gw3[:, kk, :],
                         start=(kk == 0), stop=(kk == 7),
                         after=[trts[j], self.tk_gw, self.lr[gkey]],
                         inc=("t" if kk == 7 else None))
            tlgc = OP("vector", "tensor_tensor", lg, gpb[:, 0:NE],
                      self.gbb[:, :], ALU.add, after=[tlg, self.tk_gb], inc="v")
            self.lr[gkey] = tlgc
            tm1 = tsc("vector", self.scr8[:, :], lg, 1.0, None, ALU.mult, ALU.max,
                      accum=self.col(20), inc="v")
            OP("vector", "tensor_tensor", is1, lg,
               self.col(20).broadcast_to([128, 8]), ALU.is_ge, after=[tm1])
            OP("vector", "scalar_tensor_tensor", maskd, is1, -1e30, lg,
               ALU.mult, ALU.add)
            tm2 = tsc("vector", self.scr8[:, :], maskd, 1.0, None, ALU.mult, ALU.max,
                      accum=self.col(21), inc="v")
            OP("vector", "tensor_tensor", is2, maskd,
               self.col(21).broadcast_to([128, 8]), ALU.is_ge, after=[tm2])
            td = OP("vector", "tensor_tensor", self.col(22), self.col(21),
                    self.col(20), ALU.subtract, inc="v")
            te2 = OP("scalar", "activation", self.col(23), self.col(22), AF.Exp,
                     after=[td], inc="s")
            tdn = tsc("vector", self.col(24), self.col(23), 1.0, None, ALU.add,
                      after=[te2], inc="v")
            tw1 = OP("vector", "reciprocal", self.col(25), self.col(24),
                     after=[tdn], inc="v")
            tw2 = OP("vector", "tensor_tensor", self.col(26), self.col(23),
                     self.col(25), ALU.mult, after=[tw1], inc="v")
            OP("vector", "tensor_tensor", self.G_s[:, 8 * j:8 * (j + 1)], is1,
               self.col(25).broadcast_to([128, 8]), ALU.mult, after=[tw2])
            OP("vector", "tensor_tensor", self.scr8[:, :], is2,
               self.col(26).broadcast_to([128, 8]), ALU.mult)
            g_tok = OP("vector", "tensor_tensor", self.G_s[:, 8 * j:8 * (j + 1)],
                       self.G_s[:, 8 * j:8 * (j + 1)], self.scr8[:, :], ALU.add,
                       inc="v")
            if self.dbg:
                self.dbg_toks.append(OP(
                    "sync", "dma_start",
                    out=self.dg_h[1024 * q + 128 * j:1024 * q + 128 * (j + 1), :],
                    in_=self.G_s[:, 8 * j:8 * (j + 1)], after=[g_tok], inc=("dx", 16)))
        self.dbg_t1_tok = None
        if self.dbg:
            self.dbg_t1_tok = OP(
                "sync", "dma_start",
                out=self.dt1_h.rearrange("(c p) w -> p c w", p=128)[:, 8 * q:8 * (q + 1), :],
                in_=self.t1.rearrange("p (c w) -> p c w", c=8),
                after=[self.ln_tok[7], trt], inc=("dx", 16))
            self.dbg_toks.append(self.dbg_t1_tok)
        self.load_act(self.SET_SILU)

        st_toks = []
        acc_tok = list(self.ln_tok)
        h0_banks = [self.pb2, self.pb6]
        h0_free = [self.lr["pb2"], self.lr["pb6"]]
        h1_free = self.lr["pb3"]
        out_banks = [self.pbA, self.pbB]
        out_free = [self.lr["pb01"], self.lr["pbBlin"]]
        for e in range(NE):
            lastuse = {"w00": None, "w01": None, "w10": None, "w11": None,
                       "w2": None, "b": None}
            for g2 in range(2):
                act_toks = []
                for hs in range(16):
                    ib = hs % 2
                    pbh0 = h0_banks[ib]
                    wh = "0" if hs < 8 else "1"
                    th0 = None
                    for kk in range(8):
                        th0 = OP("tensor", "matmul", pbh0[:, :],
                                 pw03[:, kk, 128 * hs:128 * (hs + 1)],
                                 t1T3[:, kk, 512 * g2:512 * (g2 + 1)],
                                 start=(kk == 0), stop=(kk == 7),
                                 after=[wtok["w0" + wh], trts[4 * g2 + 3],
                                        h0_free[ib]],
                                 inc=("t" if kk == 7 else None))
                    lastuse["w0" + wh] = th0
                    th1 = None
                    for kk in range(8):
                        th1 = OP("tensor", "matmul", self.pb3[:, :],
                                 pw13[:, kk, 128 * hs:128 * (hs + 1)],
                                 t1T3[:, kk, 512 * g2:512 * (g2 + 1)],
                                 start=(kk == 0), stop=(kk == 7),
                                 after=[wtok["w1" + wh], h1_free],
                                 inc=("t" if kk == 7 else None))
                    lastuse["w1" + wh] = th1
                    tsi = OP("scalar", "activation", self.silu_t[ib][:, :],
                             self.pb3[:, :], AF.Silu,
                             bias=self.b1col[:, hs:hs + 1],
                             after=[th1, wtok["b1"], h0_free[ib]], inc="s")
                    h1_free = tsi
                    thb = OP("scalar", "activation", self.h0b_t[ib][:, :],
                             pbh0[:, :], AF.Identity,
                             bias=self.b0col[:, hs:hs + 1],
                             after=[th0, wtok["b0"], h0_free[ib]], inc="s")
                    lastuse["b"] = thb
                    tac = OP("vector", "tensor_tensor", act3[:, hs, :],
                             self.h0b_t[ib][:, :], self.silu_t[ib][:, :], ALU.mult,
                             after=[tsi, thb, self.prev_w2g], inc="v")
                    h0_free[ib] = tac
                    act_toks.append(tac)
                tw2last = None
                for jj in range(4):
                    j = 4 * g2 + jj
                    ob = (e * 8 + g2 * 4 + jj) % 2
                    pbo = out_banks[ob]
                    to = None
                    for hs in range(16):
                        for nh in range(2):
                            to = OP("tensor", "matmul",
                                    pbo[:, 512 * nh:512 * (nh + 1)],
                                    act3[:, hs, 128 * jj:128 * (jj + 1)],
                                    pw23[:, hs, 512 * nh:512 * (nh + 1)],
                                    start=(hs == 0), stop=False,
                                    after=[act_toks[hs], wtok["w2"], out_free[ob]])
                    for nh in range(2):
                        to = OP("tensor", "matmul",
                                pbo[:, 512 * nh:512 * (nh + 1)],
                                self.r0ones[:, 0:128],
                                pw23[:, 16, 512 * nh:512 * (nh + 1)],
                                start=False, stop=True,
                                after=[self.tk_r0],
                                inc=("t" if nh == 1 else None))
                    tsc2 = OP("scalar", "activation", self.fscrB[:, :], pbo[:, :],
                              AF.Identity,
                              scale=self.G_s[:, 8 * j + e:8 * j + e + 1],
                              after=[to, g_tok, self.prev_tad], inc="s")
                    tad = OP("vector", "tensor_tensor", self.t1[:, D * j:D * (j + 1)],
                             self.t1[:, D * j:D * (j + 1)], self.fscrB[:, :], ALU.add,
                             after=[tsc2, acc_tok[j], self.dbg_t1_tok], inc="v")
                    acc_tok[j] = tad
                    out_free[ob] = tad
                    self.prev_tad = tad
                    tw2last = to
                    if e == NE - 1:
                        if jj == 0:
                            self.load_act(self.SET_NLE)
                        st_toks.append(self.final_ln_chunk(q, j, acc_tok[j]))
                self.prev_w2g = tw2last
                lastuse["w2"] = tw2last
                if e == NE - 1 and g2 == 0:
                    self.load_act(self.SET_SILU)
            if e + 1 < NE:
                wtok = load_expert(e + 1, lastuse["w00"], lastuse["w01"],
                                   lastuse["w10"], lastuse["w11"],
                                   lastuse["w2"], lastuse["b"])
            else:
                last_e = lastuse
        self.lr["pb01"] = out_free[0]
        self.lr["pbBwin"] = out_free[1]
        self.lr["pbBlin"] = out_free[1]
        self.lr["pb2"] = h0_free[0]
        self.lr["pb6"] = h0_free[1]
        self.lr["pb3"] = h1_free
        if q + 1 < self.nq:
            self.tk_wq = self.load_mat(self.wq_s, self.wq_h, 8,
                                       after=[last_e["w01"]])
            self.tk_wk = self.load_mat(self.wk_s, self.wk_h, 8,
                                       after=[last_e["w01"]])
            self.tk_wv = self.load_mat(self.wv_s, self.wv_h, 8,
                                       after=[last_e["w11"]])
            wo3h = self.wout_s.rearrange("p (s w) -> p s w", s=16)
            t_wo1 = self.OP("gpsimd", "dma_start", out=wo3h[:, 0:8, :],
                            in_=self.wout_h[0:1024, :].rearrange("(s p) w -> p s w", p=128),
                            after=[last_e["w11"]], inc=("dw", 16))
            t_wo2 = self.OP("gpsimd", "dma_start", out=wo3h[:, 8:16, :],
                            in_=self.wout_h[1024:2048, :].rearrange("(s p) w -> p s w", p=128),
                            after=[last_e["w2"]], inc=("dw", 16))
            self.tk_wo = t_wo2
        self.store_tok = st_toks[-1]
        self.t1_war = self.store_tok
        if q + 1 < self.nq:
            self.pl.op("sync", "nop", after=[self.store_tok])
            self.pl.barrier()
            OP = self.OP
            OP("vector", "memset", self.qphiT2[:, :], 0.0)
            OP("vector", "memset", self.kphiT2[:, :], 0.0)
            OP("vector", "memset",
               self.v_sup.rearrange("p (h c) -> p h c", c=65)[:, :, 64:65], 1.0,
               inc="v")
            self.prev_tts = None
            self.prev_kd = None
            self.prev_yl = None
            self.prev_yw = None
            self.prev_ct = None
            self.prev_wo = None
            self.prev_qpad_read = None
            self.prev_phig = None
            self.sup_done = None

    def build(self):
        self.t1_war = None
        self.prev_w2g = None
        self.prev_tad = None
        self.ktail_tok = None
        self.vtail_tok = None
        self.prelude()
        for q in range(self.nq):
            if "A" in self.parts:
                self.superchunk(2 * q)
                if self.nsup is None or self.nsup > 1:
                    self.superchunk(2 * q + 1)
            if "B" in self.parts:
                self.phase_B(q)
            else:
                self.store_tok = self.pl.op(
                    "sync", "dma_start",
                    out=self.out_h.rearrange("(c p) w -> p c w", p=128)[:, 8 * q:8 * (q + 1), :],
                    in_=self.t1.rearrange("p (c w) -> p c w", c=8),
                    after=[tk for tk in self.ln_tok if tk is not None], inc=("do", 16))
        self.pl.op("sync", "nop", after=[("do", self.pl.cnt["do"])])
        if self.dbg:
            self.pl.op("sync", "nop", after=[("dx", self.pl.cnt["dx"])])

        block = self.ctx.enter_context(self.nc.Block())
        pl = self.pl

        def runner(name):
            def _(eng):
                for f in pl.q[name]:
                    f(eng)
            return _
        block.tensor(runner("tensor"))
        block.vector(runner("vector"))
        block.scalar(runner("scalar"))
        block.gpsimd(runner("gpsimd"))
        block.sync(runner("sync"))
        self.ctx.close()
        return self.nc


def build_program(n_quarters=4, dbg=False, parts="AB", nsup=None):
    return Builder(n_quarters=n_quarters, dbg=dbg, parts=parts, nsup=nsup).build()


def _w2t_ext(inputs):
    w2t = np.asarray(inputs["L0_ffn_w2"]).transpose(0, 2, 1)
    ext = np.zeros((NE, 17 * 128, D), dtype=np.float32)
    ext[:, :HID, :] = w2t
    ext[:, HID, :] = np.asarray(inputs["L0_ffn_b2"])
    return np.ascontiguousarray(ext)


def _prep_inputs(inputs):
    f = np.ascontiguousarray
    x = np.asarray(inputs["L0_attn_x"])
    shared = {
        "wq": f(np.asarray(inputs["L0_attn_Wq"])),
        "wk": f(np.asarray(inputs["L0_attn_Wk"])),
        "wv": f(np.asarray(inputs["L0_attn_Wv"])),
        "wqf": f(np.asarray(inputs["L0_attn_Wqf"])),
        "wkf": f(np.asarray(inputs["L0_attn_Wkf"])),
        "wout": f(np.asarray(inputs["L0_attn_Wout"])),
        "normw": f(np.asarray(inputs["L0_attn_norm_w"]).reshape(1, D)),
        "w0t": f(np.asarray(inputs["L0_ffn_w0"]).transpose(0, 2, 1)),
        "w1t": f(np.asarray(inputs["L0_ffn_w1"]).transpose(0, 2, 1)),
        "w2t": _w2t_ext(inputs),
        "b0r": f(np.asarray(inputs["L0_ffn_b0"]).reshape(NE, 16, 128).transpose(0, 2, 1)),
        "b1r": f(np.asarray(inputs["L0_ffn_b1"]).reshape(NE, 16, 128).transpose(0, 2, 1)),
        "gwt": f(np.asarray(inputs["L0_ffn_gate_w"]).T),
        "gb": f(np.asarray(inputs["L0_ffn_gate_b"]).reshape(NE, 1)),
    }
    in_maps = []
    for c in range(8):
        m = dict(shared)
        m["x"] = f(x[c])
        in_maps.append(m)
    return in_maps


def kernel(**inputs):
    in_maps = _prep_inputs(inputs)
    nc = build_program()
    res = run_bass_kernel_spmd(nc, in_maps, core_ids=list(range(8)))
    out = np.stack([np.asarray(r["out"]) for r in res.results], axis=0)
    return out.astype(np.float32)
